# version 03
# baseline (speedup 1.0000x reference)
import os
import contextlib
import numpy as np
import concourse.bass as bass
import concourse.mybir as mybir
from concourse.alu_op_type import AluOpType
from concourse.bass_utils import run_bass_kernel_spmd

F32 = mybir.dt.float32
BF16 = mybir.dt.bfloat16
AF = mybir.ActivationFunctionType
MUL, ADD, SUB = AluOpType.mult, AluOpType.add, AluOpType.subtract
MAXOP, MINOP, ABSMAX = AluOpType.max, AluOpType.min, AluOpType.abs_max
AXF = mybir.AxisListType.X

MAGIC = float(1.5 * 2 ** 23)

B4, TSEQ, DM = 4, 4096, 1024
TL = 2048
NH, DK, HV = 8, 64, 512
DI, NST, DTR = 2048, 16, 64
FFI = 4096
NCD = DM // 128
NCI = DI // 128
NCF = FFI // 128
EPS = 1e-5

DBG = os.environ.get("BASSDBG", "")



def _stage_core(inp, c):
    p, b = c & 1, c >> 1
    d = {}
    x = np.asarray(inp["L0_attn_a_x"][b])
    d["x_in"] = np.ascontiguousarray(x[:TL] if p == 0 else x[TL:][::-1])
    hsl = slice(8 * p, 8 * p + 8)
    for nm, w in [("wq", inp["L0_attn_a_Wq"]), ("wk", inp["L0_attn_a_Wk"]),
                  ("wv", inp["L0_attn_a_Wv"])]:
        d[nm] = np.ascontiguousarray(np.asarray(w)[hsl].transpose(1, 0, 2).reshape(DM, HV))
    d["wo"] = np.ascontiguousarray(inp["L0_attn_a_Wo"])
    d["bo"] = np.ascontiguousarray(np.asarray(inp["L0_attn_a_bo"]).reshape(NCD, 128).T)
    d["w_gate"] = np.ascontiguousarray(np.asarray(inp["L0_ffn_w_gate"]).T)
    d["w_up"] = np.ascontiguousarray(np.asarray(inp["L0_ffn_w_up"]).T)
    d["w_down"] = np.ascontiguousarray(np.asarray(inp["L0_ffn_w_down"]).T)
    pref = ["f", "b"] if p == 0 else ["b", "f"]
    for i, pr in enumerate(pref, 1):
        g = lambda k: np.asarray(inp[f"L0_attn_b_{pr}_{k}"])
        d[f"in_w{i}"] = np.ascontiguousarray(g("in_w").T)
        xp = g("xproj_w")
        d[f"xpw_dt{i}"] = np.ascontiguousarray(xp[:DTR].T)
        d[f"xpw_B{i}"] = np.ascontiguousarray(xp[DTR:DTR + NST].T)
        d[f"xpw_C{i}"] = np.ascontiguousarray(xp[DTR + NST:].T)
        d[f"dt_w{i}"] = np.ascontiguousarray(g("dt_w").T)
        d[f"out_w{i}"] = np.ascontiguousarray(g("out_w").T)
        d[f"convw{i}"] = np.ascontiguousarray(
            g("conv_w").reshape(NCI, 128, 4).transpose(1, 0, 2).reshape(128, NCI * 4))
        d[f"convb{i}"] = np.ascontiguousarray(g("conv_b").reshape(NCI, 128).T)
        d[f"dtb{i}"] = np.ascontiguousarray(g("dt_b").reshape(NCI, 128).T)
        d[f"Dp{i}"] = np.ascontiguousarray(g("D").reshape(NCI, 128).T)
        d[f"Alog{i}"] = np.ascontiguousarray(
            g("A_log").reshape(NCI, 128, NST).transpose(1, 0, 2).reshape(128, NCI * NST))
    d["m0col"] = np.full((128, 1), 1.0 if p == 0 else 0.0, np.float32)
    d["m1col"] = np.full((128, 1), 0.0 if p == 0 else 1.0, np.float32)
    d["identf"] = np.eye(128, dtype=np.float32)
    s16 = np.zeros((NST, NST, 128), np.float32)
    for n in range(NST):
        s16[n, n] = 1.0
    d["sel16"] = np.ascontiguousarray(s16.transpose(1, 0, 2).reshape(NST, NST * 128))
    s72 = np.zeros((8, 72, 64), np.float32)
    for h in range(8):
        s72[h, 64 + h] = 1.0
    d["sel72"] = np.ascontiguousarray(s72.transpose(1, 0, 2).reshape(72, 8 * 64))
    s8 = np.zeros((8, 128), np.float32)
    s8[0] = 1.0
    d["sel8f"] = s8
    kk = np.arange(128)[:, None]
    uu = np.arange(-384, 512)[None, :]
    d["mask_c"] = (uu >= kk).astype(np.float32)
    d["mask_a"] = (uu <= kk).astype(np.float32)
    return d


def _in_specs():
    sp = {
        "x_in": [TL, DM], "wq": [DM, HV], "wk": [DM, HV], "wv": [DM, HV],
        "wo": [DM, DM], "bo": [128, NCD],
        "w_gate": [DM, FFI], "w_up": [DM, FFI], "w_down": [FFI, DM],
        "m0col": [128, 1], "m1col": [128, 1], "identf": [128, 128],
        "sel16": [NST, NST * 128], "sel72": [72, 8 * 64], "sel8f": [8, 128],
        "mask_c": [128, 896], "mask_a": [128, 896],
    }
    for i in (1, 2):
        sp.update({
            f"in_w{i}": [DM, 2 * DI], f"xpw_dt{i}": [DI, DTR], f"xpw_B{i}": [DI, NST],
            f"xpw_C{i}": [DI, NST], f"dt_w{i}": [DTR, DI], f"out_w{i}": [DI, DM],
            f"convw{i}": [128, NCI * 4], f"convb{i}": [128, NCI],
            f"dtb{i}": [128, NCI], f"Dp{i}": [128, NCI], f"Alog{i}": [128, NCI * NST],
        })
    return sp


def flash_sched():
    sched = []
    for qb in range(8):
        q0 = qb * 512
        tiles = []
        if qb < 4:
            for kt in range(4 * qb + 4):
                dd = kt * 128 - q0
                tiles.append((kt, ("c", dd // 128) if dd >= 0 else None))
        else:
            for kt in range(16):
                tiles.append((kt, None))
            for kt in range(16, 32):
                dd = kt * 128 - q0
                if dd <= -128:
                    tiles.append((kt, "skip"))
                elif dd < 512:
                    tiles.append((kt, ("a", dd // 128)))
                else:
                    tiles.append((kt, None))
        sched.append([t for t in tiles if t[1] != "skip"])
    return sched




def _rt(src_ap, p=128):
    return src_ap.rearrange("(a p) f -> p a f", p=p)


def _ft(sb_ap, a, f):
    return sb_ap.rearrange("p (a f) -> p a f", a=a)


class KB:

    def __init__(self):
        self.nc = bass.Bass("TRN2", target_bir_lowering=False)
        self.es = contextlib.ExitStack()
        self.es.__enter__()
        self.n = {}
        self.sem = {}
        self.block = None
        self.ehist = {"v": {"w": {}, "r": {}}, "s": {"w": {}, "r": {}}}
        self.auto = {"v": "vp", "s": "sp"}

    def mksem(self, names):
        for nm in names:
            self.sem[nm] = self.es.enter_context(self.nc.semaphore(nm))
            self.n[nm] = 0

    DMA_SEMS = ("dl", "dw", "dz", "dst", "gl", "gw", "gz")

    def inc(self, instr, name, k=None):
        if isinstance(instr, _InstrWrap):
            assert name == self.auto[instr._en], (name, instr._en)
            return self.n[name]
        if k is None:
            k = 16 if name in self.DMA_SEMS else 1
        self.n[name] += k
        instr.then_inc(self.sem[name], k)
        return self.n[name]

    def wait(self, eng, name, val=None):
        if val is None:
            val = self.n[name]
        assert val <= self.n[name], (name, val, self.n[name])
        eng.wait_ge(self.sem[name], val)

    OP_SIG = {
        "tensor_tensor": (("out",), ("in0", "in1")),
        "tensor_scalar": (("out", "accum_out"), ("in0", "scalar1", "scalar2")),
        "scalar_tensor_tensor": (("out", "accum_out"), ("in0", "scalar", "in1")),
        "tensor_copy": (("out",), ("in_",)),
        "tensor_reduce": (("out",), ("in_",)),
        "reduce_sum": (("out",), ("in_",)),
        "reduce_max": (("out",), ("in_",)),
        "tensor_tensor_scan": (("out",), ("data0", "data1", "initial")),
        "memset": (("ap",), ()),
        "reciprocal": (("out",), ("in_",)),
        "activation": (("out", "accum_out"), ("in_", "bias", "scale")),
        "copy": (("out",), ("in_",)),
    }
    OP_POS = {
        "tensor_tensor": ("out", "in0", "in1", "op"),
        "tensor_scalar": ("out", "in0", "scalar1", "scalar2", "op0", "op1"),
        "scalar_tensor_tensor": ("out", "in0", "scalar", "in1", "op0", "op1"),
        "tensor_copy": ("out", "in_"),
        "tensor_reduce": ("out", "in_", "axis", "op"),
        "reduce_sum": ("out", "in_", "axis"),
        "reduce_max": ("out", "in_", "axis"),
        "tensor_tensor_scan": ("out", "data0", "data1", "initial", "op0", "op1"),
        "memset": ("ap", "constant"),
        "reciprocal": ("out", "in_"),
        "activation": ("out", "in_", "func"),
        "copy": ("out", "in_"),
    }

    class _Proxy:
        def __init__(self, kb, eng, ename):
            self._kb, self._eng, self._en = kb, eng, ename

        def __getattr__(self, nm):
            eng = self._eng
            if nm not in KB.OP_SIG:
                return getattr(eng, nm)
            kb = self._kb
            outp, inp = KB.OP_SIG[nm]
            pos = KB.OP_POS[nm]

            def call(*args, **kwargs):
                kw = dict(zip(pos, args))
                kw.update(kwargs)
                wr, rd = set(), set()
                for p in outp:
                    a = kw.get(p)
                    if a is not None and hasattr(a, "tensor"):
                        wr.add(a.tensor.name)
                for p in inp:
                    a = kw.get(p)
                    if a is not None and hasattr(a, "tensor"):
                        rd.add(a.tensor.name)
                st = kb.ehist[self._en]
                deps = []
                for n2 in rd | wr:
                    rec = st["w"].get(n2)
                    if rec is not None:
                        deps.append(rec)
                for n2 in wr:
                    deps.extend(st["r"].get(n2, ()))
                best = {}
                for dep in deps:
                    if dep["sem"] is not None:
                        s0, c0 = dep["sem"]
                        if best.get(s0, 0) < c0:
                            best[s0] = c0
                for s0, c0 in best.items():
                    eng.wait_ge(kb.sem[s0], c0)
                instr = getattr(eng, nm)(*args, **kwargs)
                an = kb.auto[self._en]
                kb.n[an] += 1
                instr.then_inc(kb.sem[an], 1)
                rec = {"instr": instr, "sem": (an, kb.n[an])}
                for n2 in wr:
                    st["w"][n2] = rec
                    st["r"][n2] = []
                for n2 in rd:
                    st["r"].setdefault(n2, []).append(rec)
                return _InstrWrap(kb, instr, self._en)

            return call

    def t(self, fn):
        self.block.tensor(fn)

    def v(self, fn):
        self.block.vector(lambda eng: fn(KB._Proxy(self, eng, "v")))

    def s(self, fn):
        self.block.scalar(lambda eng: fn(KB._Proxy(self, eng, "s")))

    def g(self, fn):
        def wrapped(eng):
            for s in ("gl", "gw", "gz", "dst"):
                self.wait(eng, s)
            fn(eng)
        self.block.gpsimd(wrapped)

    def y(self, fn):
        def wrapped(eng):
            for s in ("dl", "dw"):
                self.wait(eng, s)
            fn(eng)
        self.block.sync(wrapped)


class _InstrWrap:

    def __init__(self, kb, instr, en):
        self._kb, self._instr, self._en = kb, instr, en

    def then_inc(self, sem, k):
        self._instr.then_inc(sem, k)
        st = self._kb.ehist[self._en]
        semname = None
        for nm, s in self._kb.sem.items():
            if s is sem:
                semname = nm
                break
        for rec in list(st["w"].values()):
            if rec["instr"] is self._instr and semname is not None:
                rec["sem"] = (semname, self._kb.n[semname])
        for lst in st["r"].values():
            for rec in lst:
                if rec["instr"] is self._instr and semname is not None:
                    rec["sem"] = (semname, self._kb.n[semname])
        return self._instr


def build_nc():
    kb = KB()
    nc = kb.nc
    es = kb.es
    kb.mksem(["dl", "dw", "dz", "dst", "gl", "gw", "gz", "tp", "vp", "sp", "gp", "cc",
              "va", "sa"])

    ins = {nm: nc.dram_tensor(nm, shape, F32, kind="ExternalInput").ap()
           for nm, shape in _in_specs().items()}
    out_t = nc.dram_tensor("out", [TL, DM], F32, kind="ExternalOutput").ap()

    dbg_outs = {}

    def dbg_decl(name, shape):
        dbg_outs[name] = nc.dram_tensor("dbg_" + name, shape, F32, kind="ExternalOutput").ap()
        return dbg_outs[name]

    scr = {}

    def dscr(name, shape, dt=BF16):
        scr[name] = nc.dram_tensor(name, shape, dt)

    dscr("xtm", [TL, DM])
    dscr("tern_g", [DM, FFI])
    dscr("tern_u", [DM, FFI])
    dscr("tern_d", [FFI, DM])
    for i in (1, 2):
        dscr(f"inw_s{i}", [DM, 2 * DI])
        dscr(f"outw_s{i}", [DI, DM])
        dscr(f"xpd_s{i}", [DI, DTR])
        dscr(f"xpb_s{i}", [DI, NST])
        dscr(f"xpc_s{i}", [DI, NST])
        dscr(f"dtw_s{i}", [DTR, DI])
        dscr(f"sz_s{i}", [NCI, 128, TL])
    dscr("xc_s", [NCI, 128, TL])
    dscr("out1_s", [NCD, 128, TL])
    dscr("xn3_s", [NCD, 128, TL])
    dscr("act2_s", [NCF, 128, TL], F32)
    dscr("xfm_s", [NCD, 128, TL])
    cc_xn1_i = nc.dram_tensor("cc_xn1_i", [NCD, 128, TL], BF16)
    cc_xn1_o = nc.dram_tensor("cc_xn1_o", [2, NCD, 128, TL], BF16)
    cc_ho_i = nc.dram_tensor("cc_ho_i", [4, 128, TSEQ], BF16)
    cc_ho_o = nc.dram_tensor("cc_ho_o", [2, 4, 128, TSEQ], BF16)
    cc_st_i = nc.dram_tensor("cc_st_i", [128, 280], F32)
    cc_st_o = nc.dram_tensor("cc_st_o", [2, 128, 280], F32)
    RG = [[0, 1], [2, 3], [4, 5], [6, 7]]

    def sb(name, shape, dt=BF16):
        return es.enter_context(nc.sbuf_tensor(name, shape, dt)).ap()

    x_t = sb("x_t", [128, NCD * TL])
    xn_t = sb("xn_t", [128, NCD * TL])
    big_t = sb("big_t", [128, NCI * 2051])
    tbuf = sb("tbuf", [128, TL])
    C = {}
    for nm in ["m0col", "m1col", "bo", "identf", "sel8f"]:
        C[nm] = sb("c_" + nm, _in_specs()[nm], F32)
    for nm in ["sel16", "sel72", "mask_c", "mask_a"]:
        C[nm] = sb("c_" + nm, _in_specs()[nm], BF16)
    identb = sb("identb", [128, 128], BF16)
    onesb = sb("onesb", [128, 1], BF16)
    MP = {}
    for i in (1, 2):
        for nm in ["convw", "convb", "dtb", "Dp", "Alog"]:
            MP[f"{nm}{i}"] = sb(f"p_{nm}{i}", _in_specs()[f"{nm}{i}"], F32)
    A_sb = {1: None, 2: None}
    rowpad = sb("rowpad", [16, TL], BF16)
    statA = sb("statA", [128, 16], F32)
    statB = sb("statB", [128, 16], F32)
    statM = sb("statM", [128, 16], F32)
    statR = sb("statR", [128, 16], F32)
    statX = sb("statX", [128, 16], F32)
    statbf = sb("statbf", [128, 16], BF16)
    repM = sb("repM", [128, TL])
    repR = sb("repR", [128, TL])
    f32row = sb("f32row", [8, 128], F32)
    wsc = sb("wsc", [128, 8], F32)

    x_ch = [x_t[:, c * TL:(c + 1) * TL] for c in range(NCD)]
    xn_ch = [xn_t[:, c * TL:(c + 1) * TL] for c in range(NCD)]

    ALLSEMS = ["dl", "dw", "dz", "dst", "gl", "gw", "gz", "tp", "vp", "sp", "cc"]

    def barrier5():
        for emit in [kb.y, kb.g, kb.t, kb.v, kb.s]:
            def f(eng):
                for s in ALLSEMS:
                    kb.wait(eng, s)
            emit(f)
    sel16 = lambda n: C["sel16"][:, n * 128:(n + 1) * 128]
    sel72 = lambda h: C["sel72"][:, h * 64:(h + 1) * 64]
    mask = lambda fam, dd: C["mask_" + fam][:, 384 - dd * 128:896 - dd * 128]

    kb.block = es.enter_context(nc.Block())
    SCHED = flash_sched()

    def p0():
        def f(sync):
            for nm in ["m0col", "m1col", "bo", "identf", "sel8f"]:
                kb.inc(sync.dma_start(out=C[nm], in_=ins[nm]), "dl")
        kb.y(f)

        def f(g):
            for nm in ["sel16", "sel72", "mask_c", "mask_a"]:
                kb.inc(g.dma_start(out=C[nm], in_=ins[nm]), "gl")
            for i in (1, 2):
                for nm in ["convw", "convb", "dtb", "Dp", "Alog"]:
                    kb.inc(g.dma_start(out=MP[f"{nm}{i}"], in_=ins[f"{nm}{i}"]), "gl")
            kb.inc(g.dma_start(out=identb, in_=ins["identf"]), "gl")
        kb.g(f)
        n_all = kb.n["gl"]

        def f(v):
            v.memset(rowpad, 0.0)
            v.memset(f32row, 0.0)
            kb.inc(v.memset(onesb, 1.0), "vp")
        kb.v(f)

        def f(s):
            kb.wait(s, "gl", n_all)
            for i in (1, 2):
                kb.inc(s.activation(out=MP[f"Alog{i}"], in_=MP[f"Alog{i}"],
                                    func=AF.Exp), "sp")
        kb.s(f)

        def f(v):
            kb.wait(v, "sp")
            for i in (1, 2):
                kb.inc(v.tensor_scalar(out=MP[f"Alog{i}"], in0=MP[f"Alog{i}"],
                                       scalar1=-1.0, scalar2=None, op0=MUL), "vp")
        kb.v(f)
        A_sb[1], A_sb[2] = MP["Alog1"], MP["Alog2"]

    def p0_x():
        pes = contextlib.ExitStack()
        xtm_sb = pes.enter_context(nc.sbuf_tensor("xtm_sb", [128, 16 * DM], BF16)).ap()

        def f(g):
            kb.inc(g.dma_start(out=_ft(xtm_sb, 16, DM), in_=_rt(ins["x_in"])), "gl")
            kb.wait(g, "gl")
            kb.inc(g.dma_start(out=_rt(scr["xtm"].ap()), in_=_ft(xtm_sb, 16, DM)), "gl")
        kb.g(f)
        nx = kb.n["gl"]

        def f(sync):
            kb.wait(sync, "gl", nx)
            for c in range(NCD):
                src = scr["xtm"].ap()[:, c * 128:(c + 1) * 128]
                kb.inc(sync.dma_start_transpose(x_ch[c], src), "dl")
        kb.y(f)
        pes.close()

    def p0_wcast():
        barrier5()
        pes = contextlib.ExitStack()
        wprep = pes.enter_context(nc.sbuf_tensor("wprep", [128, 16 * DM], BF16)).ap()

        def cast_rows(g, src, dst, rows, cols):
            a = rows // 128
            half = max(1, (16 * DM) // cols)
            for a0 in range(0, a, half):
                aa = min(half, a - a0)
                sv = _rt(src[a0 * 128:(a0 + aa) * 128, :])
                dv = _rt(dst.ap()[a0 * 128:(a0 + aa) * 128, :])
                kb.inc(g.dma_start(out=_ft(wprep[:, :aa * cols], aa, cols), in_=sv), "gw")
                kb.wait(g, "gw")
                kb.inc(g.dma_start(out=dv, in_=_ft(wprep[:, :aa * cols], aa, cols)), "gw")
                kb.wait(g, "gw")

        def f(g):
            for i in (1, 2):
                cast_rows(g, ins[f"in_w{i}"], scr[f"inw_s{i}"], DM, 2 * DI)
                cast_rows(g, ins[f"out_w{i}"], scr[f"outw_s{i}"], DI, DM)
                cast_rows(g, ins[f"xpd_s{i}".replace("_s", "w_dt") if False else f"xpw_dt{i}"], scr[f"xpd_s{i}"], DI, DTR)
                cast_rows(g, ins[f"xpw_B{i}"], scr[f"xpb_s{i}"], DI, NST)
                cast_rows(g, ins[f"xpw_C{i}"], scr[f"xpc_s{i}"], DI, NST)
                kb.inc(g.dma_start(out=wprep[:DTR, :DI], in_=ins[f"dt_w{i}"]), "gw")
                kb.wait(g, "gw")
                kb.inc(g.dma_start(out=scr[f"dtw_s{i}"].ap(), in_=wprep[:DTR, :DI]), "gw")
                kb.wait(g, "gw")
        kb.g(f)
        pes.close()

    def scalar_to_col(dstcol, ps128):
        def f(t):
            kb.wait(t, "vp")
            kb.wait(t, "sp")
            kb.inc(t.matmul(ps128[:, 0:1], C["sel8f"], f32row[0:8, 0:1],
                            start=True, stop=True), "tp")
        kb.t(f)

        def f(v):
            kb.wait(v, "tp")
            kb.inc(v.tensor_copy(out=dstcol, in_=ps128[:, 0:1]), "vp")
        kb.v(f)

    def p0_tern():
        barrier5()
        pes = contextlib.ExitStack()
        wf = pes.enter_context(nc.sbuf_tensor("wf", [128, FFI], F32)).ap()
        wqb = pes.enter_context(nc.sbuf_tensor("wqb", [128, FFI], BF16)).ap()
        acc = pes.enter_context(nc.sbuf_tensor("acc", [128, 2], F32)).ap()
        pst = PS[3][:, 0:128]

        def ternarize(wname, scrname, rows, cols, sci):
            nch = rows // 128

            def f(v):
                kb.inc(v.memset(acc, 0.0), "vp")
            kb.v(f)
            for ch in range(nch):
                bsl = wf[:, 0:cols]

                def f(sync, ch=ch, bsl=bsl):
                    kb.wait(sync, "vp")
                    kb.inc(sync.dma_start(out=bsl, in_=ins[wname][ch * 128:(ch + 1) * 128, :]), "dw")
                kb.y(f)

                def f(v, ch=ch, bsl=bsl):
                    kb.wait(v, "dw")
                    v.reduce_sum(out=acc[:, 1:2], in_=bsl, axis=AXF, apply_absolute_value=True)
                    kb.inc(v.tensor_tensor(out=acc[:, 0:1], in0=acc[:, 0:1],
                                           in1=acc[:, 1:2], op=ADD), "vp")
                kb.v(f)

            def f(t):
                kb.wait(t, "vp")
                kb.inc(t.transpose(pst[0:1, 0:128], acc[:, 0:1], C["identf"]), "tp")
            kb.t(f)

            def f(v):
                kb.wait(v, "tp")
                v.reduce_sum(out=f32row[0:1, 0:1], in_=pst[0:1, 0:128], axis=AXF)
                kb.inc(v.tensor_scalar(out=f32row[0:1, 0:1], in0=f32row[0:1, 0:1],
                                       scalar1=1.0 / (rows * cols), scalar2=None, op0=MUL), "vp")
            kb.v(f)

            scalar_to_col(wsc[:, 3 + sci:4 + sci], pst)

            def f(s):
                kb.wait(s, "vp")
                s.activation(out=f32row[0:1, 1:2], in_=f32row[0:1, 0:1], func=AF.Ln)
                kb.inc(s.activation(out=f32row[0:1, 0:1], in_=f32row[0:1, 1:2],
                                    func=AF.Exp, scale=-1.0), "sp")
            kb.s(f)

            scalar_to_col(wsc[:, sci:sci + 1], pst)

            for ch in range(nch):
                bsl = wf[:, 0:cols]

                def f(sync, ch=ch, bsl=bsl):
                    kb.wait(sync, "vp")
                    kb.inc(sync.dma_start(out=bsl, in_=ins[wname][ch * 128:(ch + 1) * 128, :]), "dw")
                kb.y(f)

                def f(v, ch=ch, bsl=bsl):
                    kb.wait(v, "dw")
                    kb.wait(v, "gz")
                    v.tensor_scalar(out=bsl, in0=bsl, scalar1=wsc[:, sci:sci + 1],
                                    scalar2=None, op0=MUL)
                    v.tensor_scalar(out=bsl, in0=bsl, scalar1=MAGIC, scalar2=MAGIC,
                                    op0=ADD, op1=SUB)
                    kb.inc(v.tensor_scalar(out=wqb[:, 0:cols], in0=bsl, scalar1=-1.0,
                                           scalar2=1.0, op0=MAXOP, op1=MINOP), "vp")
                kb.v(f)

                def f(g, ch=ch):
                    kb.wait(g, "vp")
                    kb.inc(g.dma_start(out=scr[scrname].ap()[ch * 128:(ch + 1) * 128, :],
                                       in_=wqb[:, 0:cols]), "gz")
                kb.g(f)

        ternarize("w_gate", "tern_g", DM, FFI, 0)
        ternarize("w_up", "tern_u", DM, FFI, 1)
        ternarize("w_down", "tern_d", FFI, DM, 2)

        def f(v):
            v.tensor_scalar(out=wsc[:, 6:7], in0=wsc[:, 3:4], scalar1=-1.0,
                            scalar2=None, op0=MUL)
            kb.inc(v.memset(wsc[:, 7:8], 1.0), "vp")
        kb.v(f)
        pes.close()

    PS = [es.enter_context(nc.psum_tensor(f"ps{i}", [128, 512], F32)).ap()
          for i in range(8)]
    psS = PS[0][:, 0:16]
    psQ = PS[1][:, 0:16]
    psRep = PS[2]
    psT16 = PS[3].bitcast(BF16)[:, 0:128]
    sqbuf = sb("sqbuf", [128, TL])

    def col_stats(src_ch):
        for c in range(NCD):
            def f(v, c=c):
                kb.wait(v, "tp")
                kb.inc(v.tensor_tensor(out=sqbuf, in0=src_ch[c], in1=src_ch[c],
                                       op=MUL), "vp")
            kb.v(f)

            def f(t, c=c):
                kb.wait(t, "sp")
                kb.wait(t, "vp")
                for tt in range(16):
                    kb.inc(t.matmul(psS[:, tt:tt + 1], src_ch[c][:, tt * 128:(tt + 1) * 128],
                                    onesb, start=(c == 0 and tt == 0),
                                    stop=(c == NCD - 1 and tt == 15)), "tp")
                    kb.inc(t.matmul(psQ[:, tt:tt + 1], sqbuf[:, tt * 128:(tt + 1) * 128],
                                    onesb, start=(c == 0 and tt == 0),
                                    stop=(c == NCD - 1 and tt == 15)), "tp")
            kb.t(f)

        def f(v):
            kb.wait(v, "tp")
            v.tensor_scalar(out=statM, in0=psS, scalar1=1.0 / DM, scalar2=None, op0=MUL)
            v.tensor_scalar(out=statA, in0=psQ, scalar1=1.0 / DM, scalar2=None, op0=MUL)
            v.tensor_tensor(out=statB, in0=statM, in1=statM, op=MUL)
            v.tensor_tensor(out=statA, in0=statA, in1=statB, op=SUB)
            kb.inc(v.tensor_scalar(out=statA, in0=statA, scalar1=EPS, scalar2=None,
                                   op0=ADD), "vp")
        kb.v(f)

        def f(s):
            kb.wait(s, "vp")
            s.activation(out=statB, in_=statA, func=AF.Ln)
            kb.inc(s.activation(out=statR, in_=statB, func=AF.Exp, scale=-0.5), "sp")
        kb.s(f)

    def rep_col_grouped(col_f32, dst, wait_sp=True):
        def f(v):
            if wait_sp:
                kb.wait(v, "sp")
            kb.inc(v.tensor_copy(out=statbf, in_=col_f32), "vp")
        kb.v(f)

        def f(t):
            kb.wait(t, "vp")
            kb.inc(t.transpose(psT16[0:16, :], statbf, identb), "tp")
        kb.t(f)

        def f(v):
            kb.wait(v, "tp")
            kb.inc(v.tensor_copy(out=rowpad[0:16, 0:128], in_=psT16[0:16, 0:128]), "vp")
        kb.v(f)

        for grp in range(4):
            def f(t, grp=grp):
                kb.wait(t, "vp")
                for j in range(4):
                    tt = grp * 4 + j
                    kb.inc(t.matmul(psRep[:, j * 128:(j + 1) * 128], sel16(tt),
                                    rowpad[0:16, 0:128], start=True, stop=True), "tp")
            kb.t(f)

            def f(v, grp=grp):
                kb.wait(v, "tp")
                kb.inc(v.tensor_copy(out=dst[:, grp * 512:(grp + 1) * 512], in_=psRep), "vp")
            kb.v(f)

    def ln_full(dst_ch, src_ch):
        col_stats(src_ch)
        rep_col_grouped(statM, repM)
        rep_col_grouped(statR, repR, wait_sp=False)
        for c in range(NCD):
            def f(v, c=c):
                if c == 0:
                    kb.wait(v, "vp")
                v.tensor_tensor(out=dst_ch[c], in0=src_ch[c], in1=repM, op=SUB)
                kb.inc(v.tensor_tensor(out=dst_ch[c], in0=dst_ch[c], in1=repR, op=MUL), "vp")
            kb.v(f)

    env = dict(kb=kb, nc=nc, es=es, ins=ins, out_t=out_t, scr=scr, C=C, MP=MP,
               A_sb=A_sb, x_t=x_t, xn_t=xn_t, x_ch=x_ch, xn_ch=xn_ch,
               identb=identb, onesb=onesb, rowpad=rowpad, repM=repM, repR=repR,
               statM=statM, statR=statR, statA=statA, statB=statB, statX=statX,
               statbf=statbf, f32row=f32row, wsc=wsc, sqbuf=sqbuf, tbuf=tbuf,
               PS=PS, big_t=big_t,
               sel16=sel16, sel72=sel72, mask=mask,
               cc_xn1_i=cc_xn1_i, cc_xn1_o=cc_xn1_o, cc_ho_i=cc_ho_i,
               cc_ho_o=cc_ho_o, cc_st_i=cc_st_i, cc_st_o=cc_st_o, RG=RG,
               SCHED=SCHED, ln_full=ln_full, col_stats=col_stats, barrier5=barrier5,
               rep_col_grouped=rep_col_grouped, dbg_decl=dbg_decl,
               dbg_outs=dbg_outs)

    p0()
    p0_x()
    p0_wcast()
    p0_tern()
    build_rest(env)

    es.__exit__(None, None, None)
    return nc, dbg_outs



def build_rest(env):
    kb = env["kb"]
    nc, es, ins, scr, C, MP = env["nc"], env["es"], env["ins"], env["scr"], env["C"], env["MP"]
    PS = env["PS"]
    x_ch, xn_ch, x_t, xn_t = env["x_ch"], env["xn_ch"], env["x_t"], env["xn_t"]
    big_t = env["big_t"]
    identb, onesb = env["identb"], env["onesb"]
    rowpad, repM, repR = env["rowpad"], env["repM"], env["repR"]
    statX, statB, statA = env["statX"], env["statB"], env["statA"]
    f32row, wsc = env["f32row"], env["wsc"]
    sqbuf, tbuf = env["sqbuf"], env["tbuf"]
    A_sb = env["A_sb"]
    sel16, sel72, mask = env["sel16"], env["sel72"], env["mask"]
    ln_full, col_stats, rep_col_grouped = env["ln_full"], env["col_stats"], env["rep_col_grouped"]
    cc_xn1_i, cc_xn1_o = env["cc_xn1_i"], env["cc_xn1_o"]
    cc_ho_i, cc_ho_o = env["cc_ho_i"], env["cc_ho_o"]
    cc_st_i, cc_st_o = env["cc_st_i"], env["cc_st_o"]
    RG, SCHED = env["RG"], env["SCHED"]
    barrier5 = env["barrier5"]
    out_t = env["out_t"]
    dbg_decl = env["dbg_decl"]

    def sb(name, shape, dt=BF16):
        return es.enter_context(nc.sbuf_tensor(name, shape, dt)).ap()

    def join_all(eng):
        for s in ["dl", "dw", "dz", "dst", "gl", "gw", "gz", "tp", "vp", "sp", "cc"]:
            kb.wait(eng, s)

    def dump_fm(tag, src_ap, rows=128, cols=NCD * TL):
        if not DBG:
            return
        d = dbg_decl(tag, [rows, cols])

        def f(g):
            join_all(g)
            kb.inc(g.dma_start(out=d, in_=src_ap), "dst")
            kb.wait(g, "dst")
        kb.g(f)

    def attn_stage():
        barrier5()
        ln_full(xn_ch, x_ch)
        dump_fm("xn1", xn_t)

        def f(g):
            join_all(g)
            for c in range(NCD):
                kb.inc(g.dma_start(out=cc_xn1_i.ap()[c], in_=xn_ch[c]), "gz")
            kb.wait(g, "gz")
            kb.inc(g.collective_compute(
                "AllGather", AluOpType.bypass, replica_groups=RG,
                ins=[cc_xn1_i.ap().opt()], outs=[cc_xn1_o.ap().opt()]), "cc", 1)
            kb.wait(g, "cc")
            for c in range(NCD):
                kb.inc(g.dma_start(out=big_t[:, c * TSEQ:c * TSEQ + TL],
                                   in_=cc_xn1_o.ap()[0, c]), "gl")
                kb.inc(g.dma_start(out=big_t[:, c * TSEQ + TL:(c + 1) * TSEQ],
                                   in_=cc_xn1_o.ap()[1, c]), "gl")
        kb.g(f)
        xn1f = lambda c: big_t[:, c * TSEQ:(c + 1) * TSEQ]

        with contextlib.ExitStack() as aes:
            asb = lambda n, s, dt=BF16: aes.enter_context(nc.sbuf_tensor(n, s, dt)).ap()
            wqh = asb("wqh", [128, NCD * DK])
            wkh = asb("wkh", [128, NCD * DK])
            wvh = asb("wvh", [128, NCD * DK])
            qfm = asb("qfm", [64, TSEQ])
            kfm = asb("kfm", [64, TSEQ])
            vbuf = asb("vbuf", [64, 512])
            vtm = asb("vtm", [128, 32 * 65])
            ptile = asb("ptile", [128, 2 * 512])
            hbuf = asb("hbuf", [64, TSEQ])
            lrow = asb("lrow", [72, 512])
            lrep = asb("lrep", [64, 512])

            def f(v):
                kb.wait(v, "dl")
                v.memset(lrow, 0.0)
                v.memset(vtm, 0.0)
                kb.inc(v.memset(vtm[:, 64:32 * 65:65], 1.0), "vp")
            kb.v(f)

            for h in range(NH):
                def f(g, h=h):
                    join_all(g)
                    for nm, dst in [("wq", wqh), ("wk", wkh), ("wv", wvh)]:
                        sv = _rt(ins[nm][:, h * DK:(h + 1) * DK])
                        kb.inc(g.dma_start(out=_ft(dst, NCD, DK), in_=sv), "gl")
                kb.g(f)
                nw = kb.n["gl"]

                for which, wsl, dst in [(0, wqh, qfm), (1, wkh, kfm)]:
                    for sl in range(8):
                        def f(t, wsl=wsl, sl=sl, which=which):
                            kb.wait(t, "gl", nw)
                            if which == 0 and sl == 0:
                                kb.wait(t, "vp")
                            kb.wait(t, "sp")
                            for c in range(NCD):
                                kb.inc(t.matmul(
                                    PS[0][0:64, :], wsl[:, c * DK:(c + 1) * DK],
                                    xn1f(c)[:, sl * 512:(sl + 1) * 512],
                                    start=(c == 0), stop=(c == NCD - 1)), "tp")
                        kb.t(f)

                        def f(s, dst=dst, sl=sl):
                            kb.wait(s, "tp")
                            kb.inc(s.activation(out=dst[:, sl * 512:(sl + 1) * 512],
                                                in_=PS[0][0:64, :], func=AF.Copy), "sp")
                        kb.s(f)

                for sl in range(8):
                    def f(t, sl=sl):
                        kb.wait(t, "sp")
                        for c in range(NCD):
                            kb.inc(t.matmul(
                                PS[1][0:64, :], wvh[:, c * DK:(c + 1) * DK],
                                xn1f(c)[:, sl * 512:(sl + 1) * 512],
                                start=(c == 0), stop=(c == NCD - 1)), "tp")
                    kb.t(f)

                    def f(s, sl=sl):
                        kb.wait(s, "tp")
                        kb.inc(s.activation(out=vbuf, in_=PS[1][0:64, :], func=AF.Copy), "sp")
                    kb.s(f)
                    for j in range(4):
                        tt = sl * 4 + j

                        def f(t, tt=tt, j=j):
                            kb.wait(t, "sp")
                            kb.wait(t, "vp")
                            kb.inc(t.transpose(
                                PS[2].bitcast(BF16)[:, j * 128:j * 128 + 64],
                                vbuf[0:64, j * 128:(j + 1) * 128], identb[0:64, 0:64]), "tp")
                        kb.t(f)

                        def f(v, tt=tt, j=j):
                            kb.wait(v, "tp")
                            kb.inc(v.tensor_copy(
                                out=vtm[:, tt * 65:tt * 65 + 64],
                                in_=PS[2].bitcast(BF16)[:, j * 128:j * 128 + 64]), "vp")
                        kb.v(f)

                for qb in range(8):
                    tiles = SCHED[qb]
                    nt = len(tiles)
                    exp_done = []
                    for j, (kt, mk) in enumerate(tiles):
                        def f(t, j=j, kt=kt):
                            if j == 0:
                                kb.wait(t, "vp")
                            if j >= 2:
                                kb.wait(t, "sp", exp_done[j - 2])
                            kb.inc(t.matmul(
                                PS[4 + (j % 2)], kfm[:, kt * 128:(kt + 1) * 128],
                                qfm[:, qb * 512:(qb + 1) * 512],
                                start=True, stop=True), "tp")
                        kb.t(f)

                        def f(s, j=j):
                            kb.wait(s, "tp")
                            kb.inc(s.activation(
                                out=ptile[:, (j % 2) * 512:(j % 2) * 512 + 512],
                                in_=PS[4 + (j % 2)], func=AF.Exp, scale=0.125), "sp")
                        kb.s(f)
                        exp_done.append(kb.n["sp"])
                        if mk is not None:
                            fam, dd = mk

                            def f(v, j=j, fam=fam, dd=dd):
                                kb.wait(v, "sp")
                                kb.inc(v.tensor_tensor(
                                    out=ptile[:, (j % 2) * 512:(j % 2) * 512 + 512],
                                    in0=ptile[:, (j % 2) * 512:(j % 2) * 512 + 512],
                                    in1=mask(fam, dd), op=MUL), "vp")
                            kb.v(f)

                        def f(t, j=j, kt=kt, mk=mk, nt=nt):
                            if mk is not None:
                                kb.wait(t, "vp")
                            else:
                                kb.wait(t, "sp")
                            kb.inc(t.matmul(
                                PS[6][0:65, :], vtm[:, kt * 65:(kt + 1) * 65],
                                ptile[:, (j % 2) * 512:(j % 2) * 512 + 512],
                                start=(j == 0), stop=(j == nt - 1)), "tp")
                        kb.t(f)

                    def f(v, h=h, qb=qb):
                        kb.wait(v, "tp")
                        kb.inc(v.tensor_copy(out=lrow[64:65, :],
                                             in_=PS[6][64:65, :]), "vp")
                    kb.v(f)

                    def f(s, h=h):
                        kb.wait(s, "vp")
                        s.activation(out=lrow[64:65, :],
                                     in_=lrow[64:65, :], func=AF.Ln)
                        kb.inc(s.activation(out=lrow[64:65, :],
                                            in_=lrow[64:65, :],
                                            func=AF.Exp, scale=-1.0), "sp")
                    kb.s(f)

                    def f(t, h=h):
                        kb.wait(t, "sp")
                        kb.inc(t.matmul(PS[2][0:64, :], sel72(0), lrow,
                                        start=True, stop=True), "tp")
                    kb.t(f)

                    def f(v, qb=qb):
                        kb.wait(v, "tp")
                        kb.inc(v.tensor_copy(out=lrep, in_=PS[2][0:64, :]), "vp")
                    kb.v(f)

                    def f(v, qb=qb):
                        kb.inc(v.tensor_tensor(
                            out=hbuf[:, qb * 512:(qb + 1) * 512],
                            in0=PS[6][0:64, :], in1=lrep, op=MUL), "vp")
                    kb.v(f)

                def f(g, h=h):
                    kb.wait(g, "vp")
                    kb.inc(g.dma_start(
                        out=cc_ho_i.ap()[h // 2, (h % 2) * 64:(h % 2) * 64 + 64, :],
                        in_=hbuf), "gz")
                kb.g(f)

            def f(g):
                kb.wait(g, "gz")
                kb.inc(g.collective_compute(
                    "AllGather", AluOpType.bypass, replica_groups=RG,
                    ins=[cc_ho_i.ap().opt()], outs=[cc_ho_o.ap().opt()]), "cc", 1)
                kb.wait(g, "cc")
                for rk in range(2):
                    for tile in range(4):
                        c = rk * 4 + tile
                        kb.inc(g.dma_start(out=big_t[:, c * TSEQ:(c + 1) * TSEQ],
                                           in_=cc_ho_o.ap()[rk, tile]), "gl")
            kb.g(f)

        nag2 = kb.n["gl"]
        for c in range(NCD):
            def f(v, c=c):
                if c == 0:
                    kb.wait(v, "gl", nag2)
                hoall = big_t[:, c * TSEQ:(c + 1) * TSEQ]
                v.tensor_scalar(out=xn_ch[c], in0=hoall[:, 0:TL],
                                scalar1=C["m0col"][:, 0:1], scalar2=None, op0=MUL)
                kb.inc(v.scalar_tensor_tensor(
                    out=xn_ch[c], in0=hoall[:, TL:TSEQ], scalar=C["m1col"][:, 0:1],
                    in1=xn_ch[c], op0=MUL, op1=ADD), "vp")
            kb.v(f)

        with contextlib.ExitStack() as oes:
            wo_sb = oes.enter_context(nc.sbuf_tensor("wo_sb", [128, NCD * DM], BF16)).ap()

            def f(g):
                join_all(g)
                kb.inc(g.dma_start(out=_ft(wo_sb, NCD, DM), in_=_rt(ins["wo"])), "gl")
            kb.g(f)
            nwo = kb.n["gl"]
            for dmc in range(NCD):
                for sl in range(4):
                    def f(t, dmc=dmc, sl=sl):
                        kb.wait(t, "gl", nwo)
                        if not (dmc == 0 and sl == 0):
                            kb.wait(t, "vp")
                        for c in range(NCD):
                            kb.inc(t.matmul(
                                PS[0], wo_sb[:, c * DM + dmc * 128:c * DM + dmc * 128 + 128],
                                xn_t[:, c * TL + sl * 512:c * TL + (sl + 1) * 512],
                                start=(c == 0), stop=(c == NCD - 1)), "tp")
                    kb.t(f)

                    def f(v, dmc=dmc, sl=sl):
                        kb.wait(v, "tp")
                        xsl = x_ch[dmc][:, sl * 512:(sl + 1) * 512]
                        kb.inc(v.scalar_tensor_tensor(
                            out=xsl, in0=PS[0], scalar=C["bo"][:, dmc:dmc + 1],
                            in1=xsl, op0=ADD, op1=ADD), "vp")
                    kb.v(f)
        dump_fm("x2", x_t)

    def ffn_stage(tag):
        barrier5()
        col_stats(x_ch)
        rep_col_grouped(env["statM"], repM)
        for c in range(NCD):
            def f(v, c=c):
                if c == 0:
                    kb.wait(v, "vp")
                kb.inc(v.tensor_tensor(out=xn_ch[c], in0=x_ch[c], in1=repM, op=SUB), "vp")
            kb.v(f)

        with contextlib.ExitStack() as fes:
            fsb = lambda n, s, dt=BF16: fes.enter_context(nc.sbuf_tensor(tag + "_" + n, s, dt)).ap()
            cmax = sqbuf
            qf32 = fsb("qf32", [128, 512], F32)
            tgT = fsb("tgT", [128, NCD * 512])
            tuT = fsb("tuT", [128, NCD * 512])
            tdT = fsb("tdT", [128, DM])
            gatebuf = fsb("gatebuf", [128, 512], F32)
            a2buf = fsb("a2buf", [128, 512], F32)
            a2ld = fsb("a2ld", [128, 512], F32)
            xq2b = fsb("xq2b", [128, 512])
            cm2 = tbuf
            s2rep = fsb("s2rep", [128, TL])
            brep = fsb("brep", [128, TL])

            def f(v):
                kb.wait(v, "vp")
                v.tensor_tensor(out=cmax, in0=xn_ch[0], in1=xn_ch[1], op=MAXOP)
                v.tensor_tensor(out=tbuf, in0=xn_ch[0], in1=xn_ch[1], op=MINOP)
                for c in range(2, NCD):
                    v.tensor_tensor(out=cmax, in0=cmax, in1=xn_ch[c], op=MAXOP)
                    v.tensor_tensor(out=tbuf, in0=tbuf, in1=xn_ch[c], op=MINOP)
                v.tensor_scalar(out=tbuf, in0=tbuf, scalar1=-1.0, scalar2=None, op0=MUL)
                v.tensor_tensor(out=cmax, in0=cmax, in1=tbuf, op=MAXOP)
            kb.v(f)
            for tt in range(16):
                def f(t, tt=tt):
                    kb.wait(t, "vp")
                    kb.inc(t.transpose(PS[3].bitcast(BF16)[:, 0:128],
                                       cmax[:, tt * 128:(tt + 1) * 128], identb), "tp")
                kb.t(f)

                def f(v, tt=tt):
                    kb.wait(v, "tp")
                    kb.inc(v.reduce_max(out=statX[:, tt:tt + 1],
                                        in_=PS[3].bitcast(BF16)[:, 0:128], axis=AXF), "vp")
                kb.v(f)
            def f(s):
                kb.wait(s, "vp")
                s.activation(out=statB, in_=statX, func=AF.Ln)
                kb.inc(s.activation(out=statB, in_=statB, func=AF.Exp, scale=-1.0), "sp")
            kb.s(f)

            def f(v):
                kb.wait(v, "sp")
                kb.inc(v.tensor_scalar(out=statB, in0=statB, scalar1=127.0,
                                       scalar2=None, op0=MUL), "vp")
            kb.v(f)

            def f(s):
                kb.wait(s, "vp")
                s.activation(out=statX, in_=statB, func=AF.Ln)
                kb.inc(s.activation(out=statX, in_=statX, func=AF.Exp, scale=-1.0), "sp")
            kb.s(f)

            def f(v):
                kb.wait(v, "sp")
                kb.inc(v.tensor_tensor(out=statX, in0=statX, in1=env["statR"],
                                       op=MUL), "vp")
            kb.v(f)
            rep_col_grouped(statB, repR, wait_sp=False)
            rep_col_grouped(statX, s2rep, wait_sp=False)

            for c in range(NCD):
                def f(v, c=c):
                    if c == 0:
                        kb.wait(v, "vp")
                    for sl in range(4):
                        ssl = slice(sl * 512, (sl + 1) * 512)
                        v.tensor_tensor(out=qf32, in0=xn_ch[c][:, ssl],
                                        in1=repR[:, ssl], op=MUL)
                        v.tensor_scalar(out=qf32, in0=qf32, scalar1=MAGIC, scalar2=MAGIC,
                                        op0=ADD, op1=SUB)
                        v.tensor_scalar(out=qf32, in0=qf32, scalar1=-128.0, scalar2=127.0,
                                        op0=MAXOP, op1=MINOP)
                        v.tensor_tensor(out=xn_ch[c][:, ssl], in0=qf32,
                                        in1=s2rep[:, ssl], op=MUL)
                kb.v(f)

            def f(v):
                kb.wait(v, "vp")
                v.memset(cm2, 0.0)
                kb.inc(v.memset(sqbuf, 0.0), "vp")
            kb.v(f)
            for ic in range(NCF):
                if ic % 4 == 0:
                    icg = ic // 4

                    def f(sync, icg=icg):
                        kb.wait(sync, "tp")
                        for c in range(NCD):
                            kb.inc(sync.dma_start(
                                out=tgT[:, c * 512:(c + 1) * 512],
                                in_=scr["tern_g"].ap()[c * 128:(c + 1) * 128,
                                                       icg * 512:(icg + 1) * 512]), "dw")
                            kb.inc(sync.dma_start(
                                out=tuT[:, c * 512:(c + 1) * 512],
                                in_=scr["tern_u"].ap()[c * 128:(c + 1) * 128,
                                                       icg * 512:(icg + 1) * 512]), "dw")
                    kb.y(f)
                ici = ic % 4
                for sl in range(4):
                    def f(t, ici=ici, sl=sl):
                        kb.wait(t, "dw")
                        kb.wait(t, "sp")
                        kb.wait(t, "vp")
                        for c in range(NCD):
                            kb.inc(t.matmul(
                                PS[0], tgT[:, c * 512 + ici * 128:c * 512 + ici * 128 + 128],
                                xn_ch[c][:, sl * 512:(sl + 1) * 512],
                                start=(c == 0), stop=(c == NCD - 1)), "tp")
                        for c in range(NCD):
                            kb.inc(t.matmul(
                                PS[1], tuT[:, c * 512 + ici * 128:c * 512 + ici * 128 + 128],
                                xn_ch[c][:, sl * 512:(sl + 1) * 512],
                                start=(c == 0), stop=(c == NCD - 1)), "tp")
                    kb.t(f)

                    def f(s, sl=sl):
                        kb.wait(s, "tp")
                        s.activation(out=gatebuf, in_=PS[0], func=AF.Exp,
                                     scale=wsc[:, 6:7])
                        s.activation(out=gatebuf, in_=gatebuf, func=AF.Ln,
                                     bias=wsc[:, 7:8])
                        kb.inc(s.activation(out=gatebuf, in_=gatebuf, func=AF.Exp,
                                            scale=-1.0), "sp")
                    kb.s(f)

                    def f(v, sl=sl):
                        kb.wait(v, "sp")
                        kb.inc(v.scalar_tensor_tensor(out=gatebuf, in0=PS[0],
                                                      scalar=wsc[:, 3:4], in1=gatebuf,
                                                      op0=MUL, op1=MUL), "vp")
                    kb.v(f)

                    def f(v, sl=sl):
                        kb.wait(v, "sp")
                        kb.wait(v, "gz")
                        v.tensor_tensor(out=a2buf, in0=PS[1], in1=gatebuf, op=MUL)
                        v.tensor_tensor(out=cm2[:, sl * 512:(sl + 1) * 512],
                                        in0=cm2[:, sl * 512:(sl + 1) * 512],
                                        in1=a2buf, op=MAXOP)
                        kb.inc(v.tensor_tensor(out=sqbuf[:, sl * 512:(sl + 1) * 512],
                                               in0=sqbuf[:, sl * 512:(sl + 1) * 512],
                                               in1=a2buf, op=MINOP), "vp")
                    kb.v(f)

                    def f(g, ic=ic, sl=sl):
                        kb.wait(g, "vp")
                        kb.inc(g.dma_start(
                            out=scr["act2_s"].ap()[ic][:, sl * 512:(sl + 1) * 512],
                            in_=a2buf), "gz")
                    kb.g(f)

            def f(v):
                v.tensor_scalar(out=sqbuf, in0=sqbuf, scalar1=-1.0, scalar2=None, op0=MUL)
                kb.inc(v.tensor_tensor(out=cm2, in0=cm2, in1=sqbuf, op=MAXOP), "vp")
            kb.v(f)
            for tt in range(16):
                def f(t, tt=tt):
                    if tt == 0:
                        kb.wait(t, "vp")
                    else:
                        kb.wait(t, "vp")
                    kb.inc(t.transpose(PS[3].bitcast(BF16)[:, 0:128],
                                       cm2[:, tt * 128:(tt + 1) * 128], identb), "tp")
                kb.t(f)

                def f(v, tt=tt):
                    kb.wait(v, "tp")
                    kb.inc(v.reduce_max(out=statX[:, tt:tt + 1],
                                        in_=PS[3].bitcast(BF16)[:, 0:128], axis=AXF), "vp")
                kb.v(f)

            def f(s):
                kb.wait(s, "vp")
                s.activation(out=statB, in_=statX, func=AF.Ln)
                kb.inc(s.activation(out=statB, in_=statB, func=AF.Exp, scale=-1.0), "sp")
            kb.s(f)

            def f(v):
                kb.wait(v, "sp")
                v.tensor_scalar(out=statB, in0=statB, scalar1=127.0, scalar2=None, op0=MUL)
                v.tensor_scalar(out=statA, in0=statX, scalar1=1.0 / 127.0, scalar2=None, op0=MUL)
                v.tensor_scalar(out=statA, in0=statA, scalar1=wsc[:, 4:5], scalar2=None, op0=MUL)
                kb.inc(v.tensor_scalar(out=statA, in0=statA, scalar1=wsc[:, 5:6],
                                       scalar2=None, op0=MUL), "vp")
            kb.v(f)
            rep_col_grouped(statB, s2rep, wait_sp=False)
            rep_col_grouped(statA, brep, wait_sp=False)

            for tb in range(4):
                for ic in range(NCF):
                    def f(sync, tb=tb, ic=ic):
                        kb.wait(sync, "vp")
                        kb.wait(sync, "tp")
                        kb.wait(sync, "gz")
                        kb.inc(sync.dma_start(
                            out=a2ld,
                            in_=scr["act2_s"].ap()[ic][:, tb * 512:(tb + 1) * 512]), "dw")
                        kb.inc(sync.dma_start(
                            out=tdT, in_=scr["tern_d"].ap()[ic * 128:(ic + 1) * 128, :]), "dw")
                    kb.y(f)

                    def f(v, tb=tb, ic=ic):
                        kb.wait(v, "dw")
                        kb.wait(v, "tp")
                        v.tensor_tensor(out=qf32[:, 0:512],
                                        in0=a2ld,
                                        in1=s2rep[:, tb * 512:(tb + 1) * 512], op=MUL)
                        v.tensor_scalar(out=qf32[:, 0:512], in0=qf32[:, 0:512],
                                        scalar1=MAGIC, scalar2=MAGIC, op0=ADD, op1=SUB)
                        kb.inc(v.tensor_scalar(out=xq2b, in0=qf32[:, 0:512],
                                               scalar1=-128.0, scalar2=127.0,
                                               op0=MAXOP, op1=MINOP), "vp")
                    kb.v(f)

                    def f(t, tb=tb, ic=ic):
                        kb.wait(t, "vp")
                        for dmc in range(NCD):
                            kb.inc(t.matmul(
                                PS[dmc], tdT[:, dmc * 128:(dmc + 1) * 128],
                                xq2b, start=(ic == 0), stop=(ic == NCF - 1)), "tp")
                    kb.t(f)
                for dmc in range(NCD):
                    def f(v, tb=tb, dmc=dmc):
                        kb.wait(v, "tp")
                        v.tensor_tensor(out=qf32[:, 0:512], in0=PS[dmc],
                                        in1=brep[:, tb * 512:(tb + 1) * 512], op=MUL)
                        xsl = x_ch[dmc][:, tb * 512:(tb + 1) * 512]
                        kb.inc(v.tensor_tensor(out=xsl, in0=xsl, in1=qf32[:, 0:512],
                                               op=ADD), "vp")
                    kb.v(f)
        dump_fm("x_" + tag, x_t)

    def mamba_stage():
        barrier5()
        ln_full(xn_ch, x_ch)
        dump_fm("xn3", xn_t)

        with contextlib.ExitStack() as mes:
            msb = lambda n, s, dt=BF16: mes.enter_context(nc.sbuf_tensor(n, s, dt)).ap()
            halo_m = msb("halo_m", [128, 24], F32)
            halo_bf = msb("halo_bf", [128, 24])
            carryT = msb("carryT", [128, 256], F32)
            stAB = msb("stAB", [128, 2 * 280], F32)
            dtfm = msb("dtfm", [64, TL])
            Bsb = msb("Bsb", [16, TL])
            Csb = msb("Csb", [16, TL])
            dtwT = msb("dtwT", [64, NCI * 128])
            xpT = msb("xpT", [128, NCI * 96])
            giT = msb("giT", [128, NCD * 256])
            otdT = msb("otdT", [128, DM])
            szbuf = msb("szbuf", [128, 512])
            o1buf = msb("o1buf", [128, 512])
            dab = msb("dab", [128, 512])
            dbu = msb("dbu", [128, 512])
            hb = msb("hb", [128, 512])
            gtmp = dbu
            brepb = msb("brepb", [128, 512])
            crepb = msb("crepb", [128, 512])
            ybuf8 = msb("ybuf8", [128, 8 * 512])
            ybufs = ([ybuf8[:, i * 512:(i + 1) * 512] for i in range(8)]
                     + [sqbuf[:, i * 512:(i + 1) * 512] for i in range(4)]
                     + [tbuf[:, i * 512:(i + 1) * 512] for i in range(4)])
            deltab = xn_t.bitcast(mybir.dt.float16)[:, 0:NCI * 512]
            ubuf = xn_t[:, NCI * 512:2 * NCI * 512]

            def f(v):
                kb.wait(v, "vp")
                for c in range(NCD):
                    v.tensor_copy(out=halo_m[:, c * 3:(c + 1) * 3],
                                  in_=xn_ch[c][:, TL - 3:TL])
            kb.v(f)

            def xh_base(dic):
                return dic * 2051

            def dir_pass(d):
                sfx = str(d)
                def f(sync):
                    kb.wait(sync, "tp")
                    kb.inc(sync.dma_start(out=dtwT, in_=scr["dtw_s" + sfx].ap()), "dw")
                    for dic in range(NCI):
                        rsl = slice(dic * 128, (dic + 1) * 128)
                        kb.inc(sync.dma_start(out=xpT[:, dic * 96:dic * 96 + 64],
                                              in_=scr["xpd_s" + sfx].ap()[rsl, :]), "dw")
                        kb.inc(sync.dma_start(out=xpT[:, dic * 96 + 64:dic * 96 + 80],
                                              in_=scr["xpb_s" + sfx].ap()[rsl, :]), "dw")
                        kb.inc(sync.dma_start(out=xpT[:, dic * 96 + 80:dic * 96 + 96],
                                              in_=scr["xpc_s" + sfx].ap()[rsl, :]), "dw")
                kb.y(f)

                for ocg in range(16):
                    def f(sync, ocg=ocg):
                        kb.wait(sync, "tp")
                        for c in range(NCD):
                            kb.inc(sync.dma_start(
                                out=giT[:, c * 256:(c + 1) * 256],
                                in_=scr["inw_s" + sfx].ap()[c * 128:(c + 1) * 128,
                                                            ocg * 256:(ocg + 1) * 256]), "dw")
                    kb.y(f)
                    for oci in range(2):
                        oc = ocg * 2 + oci
                        for sl in range(4):
                            def f(t, oci=oci, sl=sl, oc=oc):
                                kb.wait(t, "dw")
                                kb.wait(t, "vp")
                                kb.wait(t, "sp")
                                for c in range(NCD):
                                    kb.inc(t.matmul(
                                        PS[sl % 2],
                                        giT[:, c * 256 + oci * 128:c * 256 + oci * 128 + 128],
                                        xn_ch[c][:, sl * 512:(sl + 1) * 512],
                                        start=(c == 0), stop=(c == NCD - 1)), "tp")
                            kb.t(f)
                            if oc < NCI:
                                def f(v, oc=oc, sl=sl):
                                    kb.wait(v, "tp")
                                    kb.inc(v.tensor_copy(
                                        out=big_t[:, xh_base(oc) + 3 + sl * 512:
                                                  xh_base(oc) + 3 + (sl + 1) * 512],
                                        in_=PS[sl % 2]), "vp")
                                kb.v(f)
                            else:
                                def f(s, oc=oc, sl=sl):
                                    kb.wait(s, "tp")
                                    kb.wait(s, "gz")
                                    s.activation(out=szbuf, in_=PS[sl % 2],
                                                 func=AF.Exp, scale=-1.0)
                                    s.activation(out=szbuf, in_=szbuf, func=AF.Ln,
                                                 bias=wsc[:, 7:8])
                                    kb.inc(s.activation(out=szbuf, in_=szbuf,
                                                        func=AF.Exp, scale=-1.0), "sp")
                                kb.s(f)

                                def f(v, oc=oc, sl=sl):
                                    kb.wait(v, "sp")
                                    kb.inc(v.tensor_tensor(out=szbuf, in0=szbuf,
                                                           in1=PS[sl % 2], op=MUL), "vp")
                                kb.v(f)

                                def f(g, oc=oc, sl=sl):
                                    kb.wait(g, "vp")
                                    kb.inc(g.dma_start(
                                        out=scr["sz_s" + sfx].ap()[oc - NCI][:, sl * 512:(sl + 1) * 512],
                                        in_=szbuf), "gz")
                                kb.g(f)
                        if oc < NCI:
                            if d == 1:
                                def f(v, oc=oc):
                                    kb.wait(v, "vp")
                                    kb.inc(v.memset(
                                        big_t[:, xh_base(oc):xh_base(oc) + 3], 0.0), "vp")
                                kb.v(f)
                            else:
                                def f(t, oc=oc):
                                    kb.wait(t, "vp")
                                    for c in range(NCD):
                                        kb.inc(t.matmul(
                                            PS[2][:, 0:3],
                                            giT[:, c * 256 + (oc % 2) * 128:c * 256 + (oc % 2) * 128 + 128],
                                            halo_bf[:, c * 3:(c + 1) * 3],
                                            start=(c == 0), stop=(c == NCD - 1)), "tp")
                                kb.t(f)

                                def f(v, oc=oc):
                                    kb.wait(v, "tp")
                                    kb.inc(v.tensor_copy(
                                        out=big_t[:, xh_base(oc):xh_base(oc) + 3],
                                        in_=PS[2][:, 0:3]), "vp")
                                kb.v(f)
                        else:
                            pass

                for dic in range(NCI):
                    def f(v, dic=dic):
                        kb.wait(v, "gz")
                        kb.wait(v, "vp")
                        kb.wait(v, "sp")
                        b = xh_base(dic)
                        cw = MP["convw" + sfx]
                        v.tensor_scalar(out=sqbuf, in0=big_t[:, b:b + TL],
                                        scalar1=cw[:, dic * 4:dic * 4 + 1], scalar2=None, op0=MUL)
                        for k in (1, 2, 3):
                            v.tensor_scalar(out=tbuf, in0=big_t[:, b + k:b + k + TL],
                                            scalar1=cw[:, dic * 4 + k:dic * 4 + k + 1],
                                            scalar2=None, op0=MUL)
                            v.tensor_tensor(out=sqbuf, in0=sqbuf, in1=tbuf, op=ADD)
                        kb.inc(v.tensor_scalar(
                            out=big_t[:, b + 3:b + 3 + TL], in0=sqbuf,
                            scalar1=MP["convb" + sfx][:, dic:dic + 1],
                            scalar2=None, op0=ADD), "vp")
                    kb.v(f)

                    def f(s, dic=dic):
                        kb.wait(s, "vp")
                        b = xh_base(dic)
                        s.activation(out=tbuf, in_=big_t[:, b + 3:b + 3 + TL],
                                     func=AF.Exp, scale=-1.0)
                        s.activation(out=tbuf, in_=tbuf, func=AF.Ln, bias=wsc[:, 7:8])
                        kb.inc(s.activation(out=tbuf, in_=tbuf, func=AF.Exp,
                                            scale=-1.0), "sp")
                    kb.s(f)

                    def f(v, dic=dic):
                        kb.wait(v, "sp")
                        b = xh_base(dic)
                        kb.inc(v.tensor_tensor(out=big_t[:, b + 3:b + 3 + TL],
                                               in0=big_t[:, b + 3:b + 3 + TL],
                                               in1=tbuf, op=MUL), "vp")
                    kb.v(f)

                for sl in range(4):
                    def f(t, sl=sl):
                        kb.wait(t, "sp")
                        kb.wait(t, "vp")
                        for dic in range(NCI):
                            b = xh_base(dic)
                            rhs = big_t[:, b + 3 + sl * 512:b + 3 + (sl + 1) * 512]
                            kb.inc(t.matmul(PS[3][0:64, :], xpT[:, dic * 96:dic * 96 + 64],
                                            rhs, start=(dic == 0), stop=(dic == NCI - 1)), "tp")
                            kb.inc(t.matmul(PS[2][0:16, :], xpT[:, dic * 96 + 64:dic * 96 + 80],
                                            rhs, start=(dic == 0), stop=(dic == NCI - 1)), "tp")
                            kb.inc(t.matmul(PS[6][0:16, :], xpT[:, dic * 96 + 80:dic * 96 + 96],
                                            rhs, start=(dic == 0), stop=(dic == NCI - 1)), "tp")
                    kb.t(f)

                    def f(v, sl=sl):
                        kb.wait(v, "tp")
                        v.tensor_copy(out=dtfm[:, sl * 512:(sl + 1) * 512], in_=PS[3][0:64, :])
                        v.tensor_copy(out=Bsb[:, sl * 512:(sl + 1) * 512], in_=PS[2][0:16, :])
                        kb.inc(v.tensor_copy(out=Csb[:, sl * 512:(sl + 1) * 512],
                                             in_=PS[6][0:16, :]), "vp")
                    kb.v(f)

                if d == 1:
                    def f(g):
                        kb.wait(g, "vp")
                        for c in range(NCD):
                            kb.inc(g.dma_start(out=scr["xn3_s"].ap()[c], in_=xn_ch[c]), "gz")
                    kb.g(f)

                    def f(v):
                        kb.wait(v, "gz")
                        kb.inc(v.memset(carryT, 0.0), "vp")
                    kb.v(f)

                for tch in range(4):
                    t0 = tch * 512
                    for dic in range(NCI):
                        def f(t, dic=dic, t0=t0):
                            kb.wait(t, "vp")
                            kb.wait(t, "sp")
                            kb.inc(t.matmul(PS[3], dtwT[:, dic * 128:(dic + 1) * 128],
                                            dtfm[:, t0:t0 + 512], start=True, stop=True), "tp")
                        kb.t(f)

                        def f(s, dic=dic):
                            kb.wait(s, "tp")
                            kb.inc(s.activation(out=PS[4], in_=PS[3], func=AF.Exp,
                                                bias=MP["dtb" + sfx][:, dic:dic + 1]), "sp")
                        kb.s(f)

                        def f(v, dic=dic):
                            kb.wait(v, "sp")
                            kb.inc(v.tensor_scalar(out=PS[4], in0=PS[4], scalar1=1.0,
                                                   scalar2=None, op0=ADD), "vp")
                        kb.v(f)

                        def f(s, dic=dic):
                            kb.wait(s, "vp")
                            kb.inc(s.activation(out=deltab[:, dic * 512:(dic + 1) * 512],
                                                in_=PS[4], func=AF.Ln), "sp")
                        kb.s(f)

                        def f(v, dic=dic, t0=t0):
                            kb.wait(v, "sp")
                            b = xh_base(dic)
                            kb.inc(v.tensor_tensor(
                                out=ubuf[:, dic * 512:(dic + 1) * 512],
                                in0=deltab[:, dic * 512:(dic + 1) * 512],
                                in1=big_t[:, b + 3 + t0:b + 3 + t0 + 512], op=MUL), "vp")
                        kb.v(f)

                    for n in range(NST):
                        def f(t, n=n, t0=t0):
                            kb.wait(t, "vp")
                            kb.inc(t.matmul(PS[2], sel16(n), Bsb[:, t0:t0 + 512],
                                            start=True, stop=True), "tp")
                        kb.t(f)

                        def f(v, n=n):
                            kb.wait(v, "tp")
                            kb.inc(v.tensor_copy(out=brepb, in_=PS[2]), "vp")
                        kb.v(f)

                        def f(t, n=n, t0=t0):
                            kb.wait(t, "vp")
                            kb.inc(t.matmul(PS[2], sel16(n), Csb[:, t0:t0 + 512],
                                            start=True, stop=True), "tp")
                        kb.t(f)

                        def f(v, n=n):
                            kb.wait(v, "tp")
                            kb.inc(v.tensor_copy(out=crepb, in_=PS[2]), "vp")
                        kb.v(f)
                        for dic in range(NCI):
                            def f(s, dic=dic, n=n):
                                kb.wait(s, "vp")
                                kb.inc(s.activation(
                                    out=dab, in_=deltab[:, dic * 512:(dic + 1) * 512],
                                    func=AF.Exp,
                                    scale=A_sb[d][:, dic * NST + n:dic * NST + n + 1]), "sp")
                            kb.s(f)

                            def f(v, dic=dic, n=n):
                                kb.wait(v, "sp")
                                cidx = dic * NST + n
                                v.tensor_tensor(out=dbu, in0=ubuf[:, dic * 512:(dic + 1) * 512],
                                                in1=brepb, op=MUL)
                                v.tensor_tensor_scan(out=hb, data0=dab, data1=dbu,
                                                     initial=carryT[:, cidx:cidx + 1],
                                                     op0=MUL, op1=ADD)
                                v.tensor_copy(out=carryT[:, cidx:cidx + 1], in_=hb[:, 511:512])
                                ysl = ybufs[dic]
                                if n == 0:
                                    kb.inc(v.tensor_tensor(out=ysl, in0=hb, in1=crepb,
                                                           op=MUL), "vp")
                                else:
                                    v.tensor_tensor(out=gtmp, in0=hb, in1=crepb, op=MUL)
                                    kb.inc(v.tensor_tensor(out=ysl, in0=ysl, in1=gtmp,
                                                           op=ADD), "vp")
                            kb.v(f)

                    for dic in range(NCI):
                        def f(sync, dic=dic, tch=tch):
                            kb.wait(sync, "vp")
                            kb.inc(sync.dma_start(
                                out=szbuf[:, 0:512],
                                in_=scr["sz_s" + sfx].ap()[dic][:, tch * 512:(tch + 1) * 512]), "dw")
                        kb.y(f)

                        def f(t, dic=dic, t0=t0):
                            pass
                        def f(v, dic=dic, t0=t0):
                            kb.wait(v, "dw")
                            b = xh_base(dic)
                            ysl = ybufs[dic]
                            v.scalar_tensor_tensor(
                                out=ysl, in0=big_t[:, b + 3 + t0:b + 3 + t0 + 512],
                                scalar=MP["Dp" + sfx][:, dic:dic + 1], in1=ysl,
                                op0=MUL, op1=ADD)
                            kb.inc(v.tensor_tensor(out=ysl, in0=ysl, in1=szbuf[:, 0:512],
                                                   op=MUL), "vp")
                        kb.v(f)

                        def f(sync, dic=dic):
                            kb.wait(sync, "tp")
                            kb.inc(sync.dma_start(
                                out=otdT,
                                in_=scr["outw_s" + sfx].ap()[dic * 128:(dic + 1) * 128, :]), "dw")
                        kb.y(f)

                        def f(t, dic=dic):
                            kb.wait(t, "dw")
                            kb.wait(t, "vp")
                            for dmc in range(NCD):
                                kb.inc(t.matmul(
                                    PS[dmc], otdT[:, dmc * 128:(dmc + 1) * 128],
                                    ybufs[dic],
                                    start=(dic == 0), stop=(dic == NCI - 1)), "tp")
                        kb.t(f)

                    if d == 1:
                        for dmc in range(NCD):
                            def f(v, dmc=dmc):
                                kb.wait(v, "tp")
                                kb.wait(v, "gz")
                                kb.inc(v.tensor_copy(out=o1buf, in_=PS[dmc]), "vp")
                            kb.v(f)

                            def f(g, dmc=dmc, tch=tch):
                                kb.wait(g, "vp")
                                kb.inc(g.dma_start(
                                    out=scr["out1_s"].ap()[dmc][:, tch * 512:(tch + 1) * 512],
                                    in_=o1buf), "gz")
                            kb.g(f)
                    else:
                        rt0 = TL - t0 - 512
                        for dmc in range(NCD):
                            def f(sync, dmc=dmc, rt0=rt0):
                                kb.wait(sync, "vp")
                                kb.inc(sync.dma_start(
                                    out=o1buf,
                                    in_=scr["out1_s"].ap()[dmc][:, rt0:rt0 + 512]), "dw")
                            kb.y(f)

                            def f(v, dmc=dmc, rt0=rt0):
                                kb.wait(v, "tp")
                                kb.wait(v, "dw")
                                v.tensor_tensor(out=o1buf, in0=o1buf,
                                                in1=PS[dmc][:, ::-1], op=ADD)
                                xsl = x_ch[dmc][:, rt0:rt0 + 512]
                                kb.inc(v.tensor_tensor(out=xsl, in0=xsl, in1=o1buf,
                                                       op=ADD), "vp")
                            kb.v(f)

            dir_pass(1)

            def f(g):
                kb.wait(g, "vp")
                kb.wait(g, "tp")
                kb.inc(g.dma_start(out=cc_st_i.ap()[:, 0:256], in_=carryT), "gz")
                kb.inc(g.dma_start(out=cc_st_i.ap()[:, 256:280], in_=halo_m), "gz")
                kb.wait(g, "gz")
                kb.inc(g.collective_compute(
                    "AllGather", AluOpType.bypass, replica_groups=RG,
                    ins=[cc_st_i.ap().opt()], outs=[cc_st_o.ap().opt()]), "cc", 1)
                kb.wait(g, "cc")
                kb.inc(g.dma_start(out=stAB[:, 0:280], in_=cc_st_o.ap()[0]), "gl")
                kb.inc(g.dma_start(out=stAB[:, 280:560], in_=cc_st_o.ap()[1]), "gl")
            kb.g(f)

            def f(v):
                kb.wait(v, "gl")
                v.tensor_scalar(out=stAB[:, 0:280], in0=stAB[:, 0:280],
                                scalar1=C["m1col"][:, 0:1], scalar2=None, op0=MUL)
                v.scalar_tensor_tensor(out=stAB[:, 0:280], in0=stAB[:, 280:560],
                                       scalar=C["m0col"][:, 0:1], in1=stAB[:, 0:280],
                                       op0=MUL, op1=ADD)
                v.tensor_copy(out=carryT, in_=stAB[:, 0:256])
                kb.inc(v.tensor_copy(out=halo_bf, in_=stAB[:, 256:280]), "vp")
            kb.v(f)

            def f(g):
                kb.wait(g, "vp")
                for c in range(NCD):
                    kb.inc(g.dma_start(out=xn_ch[c], in_=scr["xn3_s"].ap()[c]), "gl")
            kb.g(f)
            for c in range(NCD):
                def f(v, c=c):
                    kb.wait(v, "gl")
                    v.tensor_copy(out=sqbuf, in_=xn_ch[c][:, ::-1])
                    kb.inc(v.tensor_copy(out=xn_ch[c], in_=sqbuf), "vp")
                kb.v(f)

            dir_pass(2)

        dump_fm("x4", x_t)

    def finish():
        barrier5()
        with contextlib.ExitStack() as zes:
            xout = zes.enter_context(nc.sbuf_tensor("xout", [128, DM], BF16)).ap()

            def f(g):
                kb.wait(g, "vp")
                kb.wait(g, "tp")
                for c in range(NCD):
                    kb.inc(g.dma_start(out=scr["xfm_s"].ap()[c], in_=x_ch[c]), "gz")
            kb.g(f)
            fmv = scr["xfm_s"].ap().rearrange("c p t -> (c p) t")
            for tt in range(16):
                def f(sync, tt=tt):
                    kb.wait(sync, "gz")
                    kb.wait(sync, "dst")
                    kb.inc(sync.dma_start_transpose(
                        xout, fmv[:, tt * 128:(tt + 1) * 128]), "dw")
                kb.y(f)

                def f(g, tt=tt):
                    kb.wait(g, "dw")
                    kb.inc(g.dma_start(out=out_t[tt * 128:(tt + 1) * 128, :], in_=xout), "dst")
                kb.g(f)

            def f(g):
                kb.wait(g, "dst")
            kb.g(f)

    attn_stage()
    ffn_stage("f1")
    mamba_stage()
    ffn_stage("f2")
    finish()


def kernel(**inputs):
    nc, _ = build_nc()
    in_maps = [_stage_core(inputs, c) for c in range(8)]
    res = run_bass_kernel_spmd(nc, in_maps, core_ids=list(range(8)))
    outs = []
    for c in range(8):
        o = np.asarray(res.results[c]["out"], dtype=np.float32)
        outs.append(o if (c & 1) == 0 else o[::-1])
    return np.stack([np.concatenate([outs[2 * b], outs[2 * b + 1]], 0) for b in range(4)])


# version 04 (final)
# speedup vs baseline: 1.0487x; 1.0487x over previous
import os
import contextlib
import numpy as np
import concourse.bass as bass
import concourse.mybir as mybir
from concourse.alu_op_type import AluOpType
from concourse.bass_utils import run_bass_kernel_spmd

F32 = mybir.dt.float32
BF16 = mybir.dt.bfloat16
AF = mybir.ActivationFunctionType
MUL, ADD, SUB = AluOpType.mult, AluOpType.add, AluOpType.subtract
MAXOP, MINOP, ABSMAX = AluOpType.max, AluOpType.min, AluOpType.abs_max
AXF = mybir.AxisListType.X

MAGIC = float(1.5 * 2 ** 23)

B4, TSEQ, DM = 4, 4096, 1024
TL = 2048
NH, DK, HV = 8, 64, 512
DI, NST, DTR = 2048, 16, 64
FFI = 4096
NCD = DM // 128
NCI = DI // 128
NCF = FFI // 128
EPS = 1e-5

DBG = os.environ.get("BASSDBG", "")



def _stage_core(inp, c):
    p, b = c & 1, c >> 1
    d = {}
    x = np.asarray(inp["L0_attn_a_x"][b])
    d["x_in"] = np.ascontiguousarray(x[:TL] if p == 0 else x[TL:][::-1])
    hsl = slice(8 * p, 8 * p + 8)
    for nm, w in [("wq", inp["L0_attn_a_Wq"]), ("wk", inp["L0_attn_a_Wk"]),
                  ("wv", inp["L0_attn_a_Wv"])]:
        d[nm] = np.ascontiguousarray(np.asarray(w)[hsl].transpose(1, 0, 2).reshape(DM, HV))
    d["wo"] = np.ascontiguousarray(inp["L0_attn_a_Wo"])
    d["bo"] = np.ascontiguousarray(np.asarray(inp["L0_attn_a_bo"]).reshape(NCD, 128).T)
    d["w_gate"] = np.ascontiguousarray(np.asarray(inp["L0_ffn_w_gate"]).T)
    d["w_up"] = np.ascontiguousarray(np.asarray(inp["L0_ffn_w_up"]).T)
    d["w_down"] = np.ascontiguousarray(np.asarray(inp["L0_ffn_w_down"]).T)
    pref = ["f", "b"] if p == 0 else ["b", "f"]
    for i, pr in enumerate(pref, 1):
        g = lambda k: np.asarray(inp[f"L0_attn_b_{pr}_{k}"])
        d[f"in_w{i}"] = np.ascontiguousarray(g("in_w").T)
        xp = g("xproj_w")
        d[f"xpw_dt{i}"] = np.ascontiguousarray(xp[:DTR].T)
        d[f"xpw_B{i}"] = np.ascontiguousarray(xp[DTR:DTR + NST].T)
        d[f"xpw_C{i}"] = np.ascontiguousarray(xp[DTR + NST:].T)
        d[f"dt_w{i}"] = np.ascontiguousarray(g("dt_w").T)
        d[f"out_w{i}"] = np.ascontiguousarray(g("out_w").T)
        d[f"convw{i}"] = np.ascontiguousarray(
            g("conv_w").reshape(NCI, 128, 4).transpose(1, 0, 2).reshape(128, NCI * 4))
        d[f"convb{i}"] = np.ascontiguousarray(g("conv_b").reshape(NCI, 128).T)
        d[f"dtb{i}"] = np.ascontiguousarray(g("dt_b").reshape(NCI, 128).T)
        d[f"Dp{i}"] = np.ascontiguousarray(g("D").reshape(NCI, 128).T)
        d[f"Alog{i}"] = np.ascontiguousarray(
            g("A_log").reshape(NCI, 128, NST).transpose(1, 0, 2).reshape(128, NCI * NST))
    d["m0col"] = np.full((128, 1), 1.0 if p == 0 else 0.0, np.float32)
    d["m1col"] = np.full((128, 1), 0.0 if p == 0 else 1.0, np.float32)
    d["identf"] = np.eye(128, dtype=np.float32)
    s16 = np.zeros((NST, NST, 128), np.float32)
    for n in range(NST):
        s16[n, n] = 1.0
    d["sel16"] = np.ascontiguousarray(s16.transpose(1, 0, 2).reshape(NST, NST * 128))
    s72 = np.zeros((8, 72, 64), np.float32)
    for h in range(8):
        s72[h, 64 + h] = 1.0
    d["sel72"] = np.ascontiguousarray(s72.transpose(1, 0, 2).reshape(72, 8 * 64))
    s8 = np.zeros((8, 128), np.float32)
    s8[0] = 1.0
    d["sel8f"] = s8
    kk = np.arange(128)[:, None]
    uu = np.arange(-384, 512)[None, :]
    d["mask_c"] = (uu >= kk).astype(np.float32)
    d["mask_a"] = (uu <= kk).astype(np.float32)
    return d


def _in_specs():
    sp = {
        "x_in": [TL, DM], "wq": [DM, HV], "wk": [DM, HV], "wv": [DM, HV],
        "wo": [DM, DM], "bo": [128, NCD],
        "w_gate": [DM, FFI], "w_up": [DM, FFI], "w_down": [FFI, DM],
        "m0col": [128, 1], "m1col": [128, 1], "identf": [128, 128],
        "sel16": [NST, NST * 128], "sel72": [72, 8 * 64], "sel8f": [8, 128],
        "mask_c": [128, 896], "mask_a": [128, 896],
    }
    for i in (1, 2):
        sp.update({
            f"in_w{i}": [DM, 2 * DI], f"xpw_dt{i}": [DI, DTR], f"xpw_B{i}": [DI, NST],
            f"xpw_C{i}": [DI, NST], f"dt_w{i}": [DTR, DI], f"out_w{i}": [DI, DM],
            f"convw{i}": [128, NCI * 4], f"convb{i}": [128, NCI],
            f"dtb{i}": [128, NCI], f"Dp{i}": [128, NCI], f"Alog{i}": [128, NCI * NST],
        })
    return sp


def flash_sched():
    sched = []
    for qb in range(8):
        q0 = qb * 512
        tiles = []
        if qb < 4:
            for kt in range(4 * qb + 4):
                dd = kt * 128 - q0
                tiles.append((kt, ("c", dd // 128) if dd >= 0 else None))
        else:
            for kt in range(16):
                tiles.append((kt, None))
            for kt in range(16, 32):
                dd = kt * 128 - q0
                if dd <= -128:
                    tiles.append((kt, "skip"))
                elif dd < 512:
                    tiles.append((kt, ("a", dd // 128)))
                else:
                    tiles.append((kt, None))
        sched.append([t for t in tiles if t[1] != "skip"])
    return sched




def _rt(src_ap, p=128):
    return src_ap.rearrange("(a p) f -> p a f", p=p)


def _ft(sb_ap, a, f):
    return sb_ap.rearrange("p (a f) -> p a f", a=a)


class KB:

    def __init__(self):
        self.nc = bass.Bass("TRN2", target_bir_lowering=False)
        self.es = contextlib.ExitStack()
        self.es.__enter__()
        self.n = {}
        self.sem = {}
        self.block = None
        self.ehist = {"v": {"w": {}, "r": {}}, "s": {"w": {}, "r": {}}}
        self.auto = {"v": "vp", "s": "sp"}

    def mksem(self, names):
        for nm in names:
            self.sem[nm] = self.es.enter_context(self.nc.semaphore(nm))
            self.n[nm] = 0

    DMA_SEMS = ("dl", "dw", "dz", "dst", "gl", "gw", "gz")

    def inc(self, instr, name, k=None):
        if isinstance(instr, _InstrWrap):
            assert name == self.auto[instr._en], (name, instr._en)
            return self.n[name]
        if k is None:
            k = 16 if name in self.DMA_SEMS else 1
        self.n[name] += k
        instr.then_inc(self.sem[name], k)
        return self.n[name]

    def wait(self, eng, name, val=None):
        if val is None:
            val = self.n[name]
        assert val <= self.n[name], (name, val, self.n[name])
        eng.wait_ge(self.sem[name], val)

    OP_SIG = {
        "tensor_tensor": (("out",), ("in0", "in1")),
        "tensor_scalar": (("out", "accum_out"), ("in0", "scalar1", "scalar2")),
        "scalar_tensor_tensor": (("out", "accum_out"), ("in0", "scalar", "in1")),
        "tensor_copy": (("out",), ("in_",)),
        "tensor_reduce": (("out",), ("in_",)),
        "reduce_sum": (("out",), ("in_",)),
        "reduce_max": (("out",), ("in_",)),
        "tensor_tensor_scan": (("out",), ("data0", "data1", "initial")),
        "memset": (("ap",), ()),
        "reciprocal": (("out",), ("in_",)),
        "activation": (("out", "accum_out"), ("in_", "bias", "scale")),
        "copy": (("out",), ("in_",)),
    }
    OP_POS = {
        "tensor_tensor": ("out", "in0", "in1", "op"),
        "tensor_scalar": ("out", "in0", "scalar1", "scalar2", "op0", "op1"),
        "scalar_tensor_tensor": ("out", "in0", "scalar", "in1", "op0", "op1"),
        "tensor_copy": ("out", "in_"),
        "tensor_reduce": ("out", "in_", "axis", "op"),
        "reduce_sum": ("out", "in_", "axis"),
        "reduce_max": ("out", "in_", "axis"),
        "tensor_tensor_scan": ("out", "data0", "data1", "initial", "op0", "op1"),
        "memset": ("ap", "constant"),
        "reciprocal": ("out", "in_"),
        "activation": ("out", "in_", "func"),
        "copy": ("out", "in_"),
    }

    class _Proxy:
        def __init__(self, kb, eng, ename):
            self._kb, self._eng, self._en = kb, eng, ename

        def __getattr__(self, nm):
            eng = self._eng
            if nm not in KB.OP_SIG:
                return getattr(eng, nm)
            kb = self._kb
            outp, inp = KB.OP_SIG[nm]
            pos = KB.OP_POS[nm]

            def call(*args, **kwargs):
                kw = dict(zip(pos, args))
                kw.update(kwargs)
                wr, rd = set(), set()
                for p in outp:
                    a = kw.get(p)
                    if a is not None and hasattr(a, "tensor"):
                        wr.add(a.tensor.name)
                for p in inp:
                    a = kw.get(p)
                    if a is not None and hasattr(a, "tensor"):
                        rd.add(a.tensor.name)
                st = kb.ehist[self._en]
                deps = []
                for n2 in rd | wr:
                    rec = st["w"].get(n2)
                    if rec is not None:
                        deps.append(rec)
                for n2 in wr:
                    deps.extend(st["r"].get(n2, ()))
                best = {}
                for dep in deps:
                    if dep["sem"] is not None:
                        s0, c0 = dep["sem"]
                        if best.get(s0, 0) < c0:
                            best[s0] = c0
                for s0, c0 in best.items():
                    eng.wait_ge(kb.sem[s0], c0)
                instr = getattr(eng, nm)(*args, **kwargs)
                an = kb.auto[self._en]
                kb.n[an] += 1
                instr.then_inc(kb.sem[an], 1)
                rec = {"instr": instr, "sem": (an, kb.n[an])}
                for n2 in wr:
                    st["w"][n2] = rec
                    st["r"][n2] = []
                for n2 in rd:
                    st["r"].setdefault(n2, []).append(rec)
                return _InstrWrap(kb, instr, self._en)

            return call

    def t(self, fn):
        self.block.tensor(fn)

    def v(self, fn):
        self.block.vector(lambda eng: fn(KB._Proxy(self, eng, "v")))

    def s(self, fn):
        self.block.scalar(lambda eng: fn(KB._Proxy(self, eng, "s")))

    def g(self, fn):
        def wrapped(eng):
            for s in ("gl", "gw", "gz", "dst"):
                self.wait(eng, s)
            fn(eng)
        self.block.gpsimd(wrapped)

    def y(self, fn):
        def wrapped(eng):
            for s in ("dl", "dw"):
                self.wait(eng, s)
            fn(eng)
        self.block.sync(wrapped)


class _InstrWrap:

    def __init__(self, kb, instr, en):
        self._kb, self._instr, self._en = kb, instr, en

    def then_inc(self, sem, k):
        self._instr.then_inc(sem, k)
        st = self._kb.ehist[self._en]
        semname = None
        for nm, s in self._kb.sem.items():
            if s is sem:
                semname = nm
                break
        for rec in list(st["w"].values()):
            if rec["instr"] is self._instr and semname is not None:
                rec["sem"] = (semname, self._kb.n[semname])
        for lst in st["r"].values():
            for rec in lst:
                if rec["instr"] is self._instr and semname is not None:
                    rec["sem"] = (semname, self._kb.n[semname])
        return self._instr


def build_nc():
    kb = KB()
    nc = kb.nc
    es = kb.es
    kb.mksem(["dl", "dw", "dz", "dst", "gl", "gw", "gz", "tp", "vp", "sp", "gp", "cc",
              "va", "sa"])

    ins = {nm: nc.dram_tensor(nm, shape, F32, kind="ExternalInput").ap()
           for nm, shape in _in_specs().items()}
    out_t = nc.dram_tensor("out", [TL, DM], F32, kind="ExternalOutput").ap()

    dbg_outs = {}

    def dbg_decl(name, shape):
        dbg_outs[name] = nc.dram_tensor("dbg_" + name, shape, F32, kind="ExternalOutput").ap()
        return dbg_outs[name]

    scr = {}

    def dscr(name, shape, dt=BF16):
        scr[name] = nc.dram_tensor(name, shape, dt)

    dscr("xtm", [TL, DM])
    dscr("tern_g", [DM, FFI])
    dscr("tern_u", [DM, FFI])
    dscr("tern_d", [FFI, DM])
    for i in (1, 2):
        dscr(f"inw_s{i}", [DM, 2 * DI])
        dscr(f"outw_s{i}", [DI, DM])
        dscr(f"xpd_s{i}", [DI, DTR])
        dscr(f"xpb_s{i}", [DI, NST])
        dscr(f"xpc_s{i}", [DI, NST])
        dscr(f"dtw_s{i}", [DTR, DI])
        dscr(f"sz_s{i}", [NCI, 128, TL])
    dscr("xc_s", [NCI, 128, TL])
    dscr("out1_s", [NCD, 128, TL])
    dscr("xn3_s", [NCD, 128, TL])
    dscr("act2_s", [NCF, 128, TL], F32)
    dscr("xfm_s", [NCD, 128, TL])
    cc_xn1_i = nc.dram_tensor("cc_xn1_i", [NCD, 128, TL], BF16)
    cc_xn1_o = nc.dram_tensor("cc_xn1_o", [2, NCD, 128, TL], BF16)
    cc_ho_i = nc.dram_tensor("cc_ho_i", [4, 128, TSEQ], BF16)
    cc_ho_o = nc.dram_tensor("cc_ho_o", [2, 4, 128, TSEQ], BF16)
    cc_st_i = nc.dram_tensor("cc_st_i", [128, 280], F32)
    cc_st_o = nc.dram_tensor("cc_st_o", [2, 128, 280], F32)
    RG = [[0, 1], [2, 3], [4, 5], [6, 7]]

    def sb(name, shape, dt=BF16):
        return es.enter_context(nc.sbuf_tensor(name, shape, dt)).ap()

    x_t = sb("x_t", [128, NCD * TL])
    xn_t = sb("xn_t", [128, NCD * TL])
    big_t = sb("big_t", [128, NCI * 2051])
    tbuf = sb("tbuf", [128, TL])
    C = {}
    for nm in ["m0col", "m1col", "bo", "identf", "sel8f"]:
        C[nm] = sb("c_" + nm, _in_specs()[nm], F32)
    for nm in ["sel16", "sel72", "mask_c", "mask_a"]:
        C[nm] = sb("c_" + nm, _in_specs()[nm], BF16)
    identb = sb("identb", [128, 128], BF16)
    onesb = sb("onesb", [128, 1], BF16)
    MP = {}
    for i in (1, 2):
        for nm in ["convw", "convb", "dtb", "Dp", "Alog"]:
            MP[f"{nm}{i}"] = sb(f"p_{nm}{i}", _in_specs()[f"{nm}{i}"], F32)
    A_sb = {1: None, 2: None}
    rowpad = sb("rowpad", [16, TL], BF16)
    statA = sb("statA", [128, 16], F32)
    statB = sb("statB", [128, 16], F32)
    statM = sb("statM", [128, 16], F32)
    statR = sb("statR", [128, 16], F32)
    statX = sb("statX", [128, 16], F32)
    statbf = sb("statbf", [128, 16], BF16)
    repM = sb("repM", [128, TL])
    repR = sb("repR", [128, TL])
    f32row = sb("f32row", [8, 128], F32)
    wsc = sb("wsc", [128, 8], F32)

    x_ch = [x_t[:, c * TL:(c + 1) * TL] for c in range(NCD)]
    xn_ch = [xn_t[:, c * TL:(c + 1) * TL] for c in range(NCD)]

    ALLSEMS = ["dl", "dw", "dz", "dst", "gl", "gw", "gz", "tp", "vp", "sp", "cc"]

    def barrier5():
        for emit in [kb.y, kb.g, kb.t, kb.v, kb.s]:
            def f(eng):
                for s in ALLSEMS:
                    kb.wait(eng, s)
            emit(f)
    sel16 = lambda n: C["sel16"][:, n * 128:(n + 1) * 128]
    sel72 = lambda h: C["sel72"][:, h * 64:(h + 1) * 64]
    mask = lambda fam, dd: C["mask_" + fam][:, 384 - dd * 128:896 - dd * 128]

    kb.block = es.enter_context(nc.Block())
    SCHED = flash_sched()

    def p0():
        def f(sync):
            for nm in ["m0col", "m1col", "bo", "identf", "sel8f"]:
                kb.inc(sync.dma_start(out=C[nm], in_=ins[nm]), "dl")
        kb.y(f)

        def f(g):
            for nm in ["sel16", "sel72", "mask_c", "mask_a"]:
                kb.inc(g.dma_start(out=C[nm], in_=ins[nm]), "gl")
            for i in (1, 2):
                for nm in ["convw", "convb", "dtb", "Dp", "Alog"]:
                    kb.inc(g.dma_start(out=MP[f"{nm}{i}"], in_=ins[f"{nm}{i}"]), "gl")
            kb.inc(g.dma_start(out=identb, in_=ins["identf"]), "gl")
        kb.g(f)
        n_all = kb.n["gl"]

        def f(v):
            v.memset(rowpad, 0.0)
            v.memset(f32row, 0.0)
            kb.inc(v.memset(onesb, 1.0), "vp")
        kb.v(f)

        def f(s):
            kb.wait(s, "gl", n_all)
            for i in (1, 2):
                kb.inc(s.activation(out=MP[f"Alog{i}"], in_=MP[f"Alog{i}"],
                                    func=AF.Exp), "sp")
        kb.s(f)

        def f(v):
            kb.wait(v, "sp")
            for i in (1, 2):
                kb.inc(v.tensor_scalar(out=MP[f"Alog{i}"], in0=MP[f"Alog{i}"],
                                       scalar1=-1.0, scalar2=None, op0=MUL), "vp")
        kb.v(f)
        A_sb[1], A_sb[2] = MP["Alog1"], MP["Alog2"]

    def p0_x():
        pes = contextlib.ExitStack()
        xtm_sb = pes.enter_context(nc.sbuf_tensor("xtm_sb", [128, 16 * DM], BF16)).ap()

        def f(g):
            kb.inc(g.dma_start(out=_ft(xtm_sb, 16, DM), in_=_rt(ins["x_in"])), "gl")
            kb.wait(g, "gl")
            kb.inc(g.dma_start(out=_rt(scr["xtm"].ap()), in_=_ft(xtm_sb, 16, DM)), "gl")
        kb.g(f)
        nx = kb.n["gl"]

        def f(sync):
            kb.wait(sync, "gl", nx)
            for c in range(NCD):
                src = scr["xtm"].ap()[:, c * 128:(c + 1) * 128]
                kb.inc(sync.dma_start_transpose(x_ch[c], src), "dl")
        kb.y(f)
        pes.close()

    def p0_wcast():
        barrier5()
        pes = contextlib.ExitStack()
        wprep = pes.enter_context(nc.sbuf_tensor("wprep", [128, 16 * DM], BF16)).ap()

        def cast_rows(g, src, dst, rows, cols):
            a = rows // 128
            half = max(1, (16 * DM) // cols)
            for a0 in range(0, a, half):
                aa = min(half, a - a0)
                sv = _rt(src[a0 * 128:(a0 + aa) * 128, :])
                dv = _rt(dst.ap()[a0 * 128:(a0 + aa) * 128, :])
                kb.inc(g.dma_start(out=_ft(wprep[:, :aa * cols], aa, cols), in_=sv), "gw")
                kb.wait(g, "gw")
                kb.inc(g.dma_start(out=dv, in_=_ft(wprep[:, :aa * cols], aa, cols)), "gw")
                kb.wait(g, "gw")

        def f(g):
            for i in (1, 2):
                cast_rows(g, ins[f"in_w{i}"], scr[f"inw_s{i}"], DM, 2 * DI)
                cast_rows(g, ins[f"out_w{i}"], scr[f"outw_s{i}"], DI, DM)
                cast_rows(g, ins[f"xpd_s{i}".replace("_s", "w_dt") if False else f"xpw_dt{i}"], scr[f"xpd_s{i}"], DI, DTR)
                cast_rows(g, ins[f"xpw_B{i}"], scr[f"xpb_s{i}"], DI, NST)
                cast_rows(g, ins[f"xpw_C{i}"], scr[f"xpc_s{i}"], DI, NST)
                kb.inc(g.dma_start(out=wprep[:DTR, :DI], in_=ins[f"dt_w{i}"]), "gw")
                kb.wait(g, "gw")
                kb.inc(g.dma_start(out=scr[f"dtw_s{i}"].ap(), in_=wprep[:DTR, :DI]), "gw")
                kb.wait(g, "gw")
        kb.g(f)
        pes.close()

    def scalar_to_col(dstcol, ps128):
        def f(t):
            kb.wait(t, "vp")
            kb.wait(t, "sp")
            kb.inc(t.matmul(ps128[:, 0:1], C["sel8f"], f32row[0:8, 0:1],
                            start=True, stop=True), "tp")
        kb.t(f)

        def f(v):
            kb.wait(v, "tp")
            kb.inc(v.tensor_copy(out=dstcol, in_=ps128[:, 0:1]), "vp")
        kb.v(f)

    def p0_tern():
        barrier5()
        pes = contextlib.ExitStack()
        wf = pes.enter_context(nc.sbuf_tensor("wf", [128, FFI], F32)).ap()
        wqb = pes.enter_context(nc.sbuf_tensor("wqb", [128, FFI], BF16)).ap()
        acc = pes.enter_context(nc.sbuf_tensor("acc", [128, 2], F32)).ap()
        pst = PS[3][:, 0:128]

        def ternarize(wname, scrname, rows, cols, sci):
            nch = rows // 128

            def f(v):
                kb.inc(v.memset(acc, 0.0), "vp")
            kb.v(f)
            for ch in range(nch):
                bsl = wf[:, 0:cols]

                def f(sync, ch=ch, bsl=bsl):
                    kb.wait(sync, "vp")
                    kb.inc(sync.dma_start(out=bsl, in_=ins[wname][ch * 128:(ch + 1) * 128, :]), "dw")
                kb.y(f)

                def f(v, ch=ch, bsl=bsl):
                    kb.wait(v, "dw")
                    v.reduce_sum(out=acc[:, 1:2], in_=bsl, axis=AXF, apply_absolute_value=True)
                    kb.inc(v.tensor_tensor(out=acc[:, 0:1], in0=acc[:, 0:1],
                                           in1=acc[:, 1:2], op=ADD), "vp")
                kb.v(f)

            def f(t):
                kb.wait(t, "vp")
                kb.inc(t.transpose(pst[0:1, 0:128], acc[:, 0:1], C["identf"]), "tp")
            kb.t(f)

            def f(v):
                kb.wait(v, "tp")
                v.reduce_sum(out=f32row[0:1, 0:1], in_=pst[0:1, 0:128], axis=AXF)
                kb.inc(v.tensor_scalar(out=f32row[0:1, 0:1], in0=f32row[0:1, 0:1],
                                       scalar1=1.0 / (rows * cols), scalar2=None, op0=MUL), "vp")
            kb.v(f)

            scalar_to_col(wsc[:, 3 + sci:4 + sci], pst)

            def f(s):
                kb.wait(s, "vp")
                s.activation(out=f32row[0:1, 1:2], in_=f32row[0:1, 0:1], func=AF.Ln)
                kb.inc(s.activation(out=f32row[0:1, 0:1], in_=f32row[0:1, 1:2],
                                    func=AF.Exp, scale=-1.0), "sp")
            kb.s(f)

            scalar_to_col(wsc[:, sci:sci + 1], pst)

            for ch in range(nch):
                bsl = wf[:, 0:cols]

                def f(sync, ch=ch, bsl=bsl):
                    kb.wait(sync, "vp")
                    kb.inc(sync.dma_start(out=bsl, in_=ins[wname][ch * 128:(ch + 1) * 128, :]), "dw")
                kb.y(f)

                def f(v, ch=ch, bsl=bsl):
                    kb.wait(v, "dw")
                    kb.wait(v, "gz")
                    v.tensor_scalar(out=bsl, in0=bsl, scalar1=wsc[:, sci:sci + 1],
                                    scalar2=None, op0=MUL)
                    v.tensor_scalar(out=bsl, in0=bsl, scalar1=MAGIC, scalar2=MAGIC,
                                    op0=ADD, op1=SUB)
                    kb.inc(v.tensor_scalar(out=wqb[:, 0:cols], in0=bsl, scalar1=-1.0,
                                           scalar2=1.0, op0=MAXOP, op1=MINOP), "vp")
                kb.v(f)

                def f(g, ch=ch):
                    kb.wait(g, "vp")
                    kb.inc(g.dma_start(out=scr[scrname].ap()[ch * 128:(ch + 1) * 128, :],
                                       in_=wqb[:, 0:cols]), "gz")
                kb.g(f)

        ternarize("w_gate", "tern_g", DM, FFI, 0)
        ternarize("w_up", "tern_u", DM, FFI, 1)
        ternarize("w_down", "tern_d", FFI, DM, 2)
        pes.close()

    PS = [es.enter_context(nc.psum_tensor(f"ps{i}", [128, 512], F32)).ap()
          for i in range(8)]
    psS = PS[0][:, 0:16]
    psQ = PS[1][:, 0:16]
    psRep = PS[2]
    psT16 = PS[3].bitcast(BF16)[:, 0:128]
    sqbuf = sb("sqbuf", [128, TL])

    def col_stats(src_ch):
        for c in range(NCD):
            def f(s, c=c):
                kb.wait(s, "tp")
                kb.inc(s.activation(out=sqbuf, in_=src_ch[c], func=AF.Square), "sp")
            kb.s(f)

            def f(t, c=c):
                kb.wait(t, "sp")
                kb.wait(t, "vp")
                for tt in range(16):
                    kb.inc(t.matmul(psS[:, tt:tt + 1], src_ch[c][:, tt * 128:(tt + 1) * 128],
                                    onesb, start=(c == 0 and tt == 0),
                                    stop=(c == NCD - 1 and tt == 15)), "tp")
                    kb.inc(t.matmul(psQ[:, tt:tt + 1], sqbuf[:, tt * 128:(tt + 1) * 128],
                                    onesb, start=(c == 0 and tt == 0),
                                    stop=(c == NCD - 1 and tt == 15)), "tp")
            kb.t(f)

        def f(v):
            kb.wait(v, "tp")
            v.tensor_scalar(out=statM, in0=psS, scalar1=1.0 / DM, scalar2=None, op0=MUL)
            v.tensor_scalar(out=statA, in0=psQ, scalar1=1.0 / DM, scalar2=None, op0=MUL)
            v.tensor_tensor(out=statB, in0=statM, in1=statM, op=MUL)
            v.tensor_tensor(out=statA, in0=statA, in1=statB, op=SUB)
            kb.inc(v.tensor_scalar(out=statA, in0=statA, scalar1=EPS, scalar2=None,
                                   op0=ADD), "vp")
        kb.v(f)

        def f(s):
            kb.wait(s, "vp")
            s.activation(out=statB, in_=statA, func=AF.Ln)
            kb.inc(s.activation(out=statR, in_=statB, func=AF.Exp, scale=-0.5), "sp")
        kb.s(f)

    def rep_col_grouped(col_f32, dst, wait_sp=True):
        def f(v):
            if wait_sp:
                kb.wait(v, "sp")
            kb.inc(v.tensor_copy(out=statbf, in_=col_f32), "vp")
        kb.v(f)

        def f(t):
            kb.wait(t, "vp")
            kb.inc(t.transpose(psT16[0:16, :], statbf, identb), "tp")
        kb.t(f)

        def f(v):
            kb.wait(v, "tp")
            kb.inc(v.tensor_copy(out=rowpad[0:16, 0:128], in_=psT16[0:16, 0:128]), "vp")
        kb.v(f)

        for grp in range(4):
            def f(t, grp=grp):
                kb.wait(t, "vp")
                for j in range(4):
                    tt = grp * 4 + j
                    kb.inc(t.matmul(psRep[:, j * 128:(j + 1) * 128], sel16(tt),
                                    rowpad[0:16, 0:128], start=True, stop=True), "tp")
            kb.t(f)

            def f(v, grp=grp):
                kb.wait(v, "tp")
                kb.inc(v.tensor_copy(out=dst[:, grp * 512:(grp + 1) * 512], in_=psRep), "vp")
            kb.v(f)

    def ln_full(dst_ch, src_ch):
        col_stats(src_ch)
        rep_col_grouped(statM, repM)
        rep_col_grouped(statR, repR, wait_sp=False)
        for c in range(NCD):
            def f(v, c=c):
                if c == 0:
                    kb.wait(v, "vp")
                v.tensor_tensor(out=dst_ch[c], in0=src_ch[c], in1=repM, op=SUB)
                kb.inc(v.tensor_tensor(out=dst_ch[c], in0=dst_ch[c], in1=repR, op=MUL), "vp")
            kb.v(f)

    env = dict(kb=kb, nc=nc, es=es, ins=ins, out_t=out_t, scr=scr, C=C, MP=MP,
               A_sb=A_sb, x_t=x_t, xn_t=xn_t, x_ch=x_ch, xn_ch=xn_ch,
               identb=identb, onesb=onesb, rowpad=rowpad, repM=repM, repR=repR,
               statM=statM, statR=statR, statA=statA, statB=statB, statX=statX,
               statbf=statbf, f32row=f32row, wsc=wsc, sqbuf=sqbuf, tbuf=tbuf,
               PS=PS, big_t=big_t,
               sel16=sel16, sel72=sel72, mask=mask,
               cc_xn1_i=cc_xn1_i, cc_xn1_o=cc_xn1_o, cc_ho_i=cc_ho_i,
               cc_ho_o=cc_ho_o, cc_st_i=cc_st_i, cc_st_o=cc_st_o, RG=RG,
               SCHED=SCHED, ln_full=ln_full, col_stats=col_stats, barrier5=barrier5,
               rep_col_grouped=rep_col_grouped, dbg_decl=dbg_decl,
               dbg_outs=dbg_outs)

    p0()
    p0_x()
    p0_wcast()
    p0_tern()
    build_rest(env)

    es.__exit__(None, None, None)
    return nc, dbg_outs



def build_rest(env):
    kb = env["kb"]
    nc, es, ins, scr, C, MP = env["nc"], env["es"], env["ins"], env["scr"], env["C"], env["MP"]
    PS = env["PS"]
    x_ch, xn_ch, x_t, xn_t = env["x_ch"], env["xn_ch"], env["x_t"], env["xn_t"]
    big_t = env["big_t"]
    identb, onesb = env["identb"], env["onesb"]
    rowpad, repM, repR = env["rowpad"], env["repM"], env["repR"]
    statX, statB, statA = env["statX"], env["statB"], env["statA"]
    f32row, wsc = env["f32row"], env["wsc"]
    sqbuf, tbuf = env["sqbuf"], env["tbuf"]
    A_sb = env["A_sb"]
    sel16, sel72, mask = env["sel16"], env["sel72"], env["mask"]
    ln_full, col_stats, rep_col_grouped = env["ln_full"], env["col_stats"], env["rep_col_grouped"]
    cc_xn1_i, cc_xn1_o = env["cc_xn1_i"], env["cc_xn1_o"]
    cc_ho_i, cc_ho_o = env["cc_ho_i"], env["cc_ho_o"]
    cc_st_i, cc_st_o = env["cc_st_i"], env["cc_st_o"]
    RG, SCHED = env["RG"], env["SCHED"]
    barrier5 = env["barrier5"]
    out_t = env["out_t"]
    dbg_decl = env["dbg_decl"]

    def sb(name, shape, dt=BF16):
        return es.enter_context(nc.sbuf_tensor(name, shape, dt)).ap()

    def join_all(eng):
        for s in ["dl", "dw", "dz", "dst", "gl", "gw", "gz", "tp", "vp", "sp", "cc"]:
            kb.wait(eng, s)

    def dump_fm(tag, src_ap, rows=128, cols=NCD * TL):
        if not DBG:
            return
        d = dbg_decl(tag, [rows, cols])

        def f(g):
            join_all(g)
            kb.inc(g.dma_start(out=d, in_=src_ap), "dst")
            kb.wait(g, "dst")
        kb.g(f)

    def attn_stage():
        barrier5()
        ln_full(xn_ch, x_ch)
        dump_fm("xn1", xn_t)

        def f(g):
            join_all(g)
            for c in range(NCD):
                kb.inc(g.dma_start(out=cc_xn1_i.ap()[c], in_=xn_ch[c]), "gz")
            kb.wait(g, "gz")
            kb.inc(g.collective_compute(
                "AllGather", AluOpType.bypass, replica_groups=RG,
                ins=[cc_xn1_i.ap().opt()], outs=[cc_xn1_o.ap().opt()]), "cc", 1)
            kb.wait(g, "cc")
            for c in range(NCD):
                kb.inc(g.dma_start(out=big_t[:, c * TSEQ:c * TSEQ + TL],
                                   in_=cc_xn1_o.ap()[0, c]), "gl")
                kb.inc(g.dma_start(out=big_t[:, c * TSEQ + TL:(c + 1) * TSEQ],
                                   in_=cc_xn1_o.ap()[1, c]), "gl")
        kb.g(f)
        xn1f = lambda c: big_t[:, c * TSEQ:(c + 1) * TSEQ]

        with contextlib.ExitStack() as aes:
            asb = lambda n, s, dt=BF16: aes.enter_context(nc.sbuf_tensor(n, s, dt)).ap()
            wqh = asb("wqh", [128, NCD * DK])
            wkh = asb("wkh", [128, NCD * DK])
            wvh = asb("wvh", [128, NCD * DK])
            qfm = asb("qfm", [64, TSEQ])
            kfm = asb("kfm", [64, TSEQ])
            vbuf = asb("vbuf", [64, 512])
            vtm = asb("vtm", [128, 32 * 65])
            ptile = asb("ptile", [128, 2 * 512])
            hbuf = asb("hbuf", [64, TSEQ])
            lrow = asb("lrow", [72, 512])
            lrep = asb("lrep", [64, 512])

            def f(v):
                kb.wait(v, "dl")
                v.memset(lrow, 0.0)
                v.memset(vtm, 0.0)
                kb.inc(v.memset(vtm[:, 64:32 * 65:65], 1.0), "vp")
            kb.v(f)

            for h in range(NH):
                def f(g, h=h):
                    join_all(g)
                    for nm, dst in [("wq", wqh), ("wk", wkh), ("wv", wvh)]:
                        sv = _rt(ins[nm][:, h * DK:(h + 1) * DK])
                        kb.inc(g.dma_start(out=_ft(dst, NCD, DK), in_=sv), "gl")
                kb.g(f)
                nw = kb.n["gl"]

                for which, wsl, dst in [(0, wqh, qfm), (1, wkh, kfm)]:
                    for sl in range(8):
                        def f(t, wsl=wsl, sl=sl, which=which):
                            kb.wait(t, "gl", nw)
                            if which == 0 and sl == 0:
                                kb.wait(t, "vp")
                            kb.wait(t, "sp")
                            for c in range(NCD):
                                kb.inc(t.matmul(
                                    PS[0][0:64, :], wsl[:, c * DK:(c + 1) * DK],
                                    xn1f(c)[:, sl * 512:(sl + 1) * 512],
                                    start=(c == 0), stop=(c == NCD - 1)), "tp")
                        kb.t(f)

                        def f(s, dst=dst, sl=sl):
                            kb.wait(s, "tp")
                            kb.inc(s.activation(out=dst[:, sl * 512:(sl + 1) * 512],
                                                in_=PS[0][0:64, :], func=AF.Copy), "sp")
                        kb.s(f)

                for sl in range(8):
                    def f(t, sl=sl):
                        kb.wait(t, "sp")
                        for c in range(NCD):
                            kb.inc(t.matmul(
                                PS[1][0:64, :], wvh[:, c * DK:(c + 1) * DK],
                                xn1f(c)[:, sl * 512:(sl + 1) * 512],
                                start=(c == 0), stop=(c == NCD - 1)), "tp")
                    kb.t(f)

                    def f(s, sl=sl):
                        kb.wait(s, "tp")
                        kb.inc(s.activation(out=vbuf, in_=PS[1][0:64, :], func=AF.Copy), "sp")
                    kb.s(f)
                    for j in range(4):
                        tt = sl * 4 + j

                        def f(t, tt=tt, j=j):
                            kb.wait(t, "sp")
                            kb.wait(t, "vp")
                            kb.inc(t.transpose(
                                PS[2].bitcast(BF16)[:, j * 128:j * 128 + 64],
                                vbuf[0:64, j * 128:(j + 1) * 128], identb[0:64, 0:64]), "tp")
                        kb.t(f)

                        def f(v, tt=tt, j=j):
                            kb.wait(v, "tp")
                            kb.inc(v.tensor_copy(
                                out=vtm[:, tt * 65:tt * 65 + 64],
                                in_=PS[2].bitcast(BF16)[:, j * 128:j * 128 + 64]), "vp")
                        kb.v(f)

                for qb in range(8):
                    tiles = SCHED[qb]
                    nt = len(tiles)
                    exp_done = []
                    for j, (kt, mk) in enumerate(tiles):
                        def f(t, j=j, kt=kt):
                            if j == 0:
                                kb.wait(t, "vp")
                            if j >= 2:
                                kb.wait(t, "sp", exp_done[j - 2])
                            kb.inc(t.matmul(
                                PS[4 + (j % 2)], kfm[:, kt * 128:(kt + 1) * 128],
                                qfm[:, qb * 512:(qb + 1) * 512],
                                start=True, stop=True), "tp")
                        kb.t(f)

                        def f(s, j=j):
                            kb.wait(s, "tp")
                            kb.inc(s.activation(
                                out=ptile[:, (j % 2) * 512:(j % 2) * 512 + 512],
                                in_=PS[4 + (j % 2)], func=AF.Exp, scale=0.125), "sp")
                        kb.s(f)
                        exp_done.append(kb.n["sp"])
                        if mk is not None:
                            fam, dd = mk

                            def f(v, j=j, fam=fam, dd=dd):
                                kb.wait(v, "sp")
                                kb.inc(v.tensor_tensor(
                                    out=ptile[:, (j % 2) * 512:(j % 2) * 512 + 512],
                                    in0=ptile[:, (j % 2) * 512:(j % 2) * 512 + 512],
                                    in1=mask(fam, dd), op=MUL), "vp")
                            kb.v(f)

                        def f(t, j=j, kt=kt, mk=mk, nt=nt):
                            if mk is not None:
                                kb.wait(t, "vp")
                            else:
                                kb.wait(t, "sp")
                            kb.inc(t.matmul(
                                PS[6][0:65, :], vtm[:, kt * 65:(kt + 1) * 65],
                                ptile[:, (j % 2) * 512:(j % 2) * 512 + 512],
                                start=(j == 0), stop=(j == nt - 1)), "tp")
                        kb.t(f)

                    def f(v, h=h, qb=qb):
                        kb.wait(v, "tp")
                        kb.inc(v.tensor_copy(out=lrow[64:65, :],
                                             in_=PS[6][64:65, :]), "vp")
                    kb.v(f)

                    def f(s, h=h):
                        kb.wait(s, "vp")
                        s.activation(out=lrow[64:65, :],
                                     in_=lrow[64:65, :], func=AF.Ln)
                        kb.inc(s.activation(out=lrow[64:65, :],
                                            in_=lrow[64:65, :],
                                            func=AF.Exp, scale=-1.0), "sp")
                    kb.s(f)

                    def f(t, h=h):
                        kb.wait(t, "sp")
                        kb.inc(t.matmul(PS[2][0:64, :], sel72(0), lrow,
                                        start=True, stop=True), "tp")
                    kb.t(f)

                    def f(v, qb=qb):
                        kb.wait(v, "tp")
                        kb.inc(v.tensor_copy(out=lrep, in_=PS[2][0:64, :]), "vp")
                    kb.v(f)

                    def f(v, qb=qb):
                        kb.inc(v.tensor_tensor(
                            out=hbuf[:, qb * 512:(qb + 1) * 512],
                            in0=PS[6][0:64, :], in1=lrep, op=MUL), "vp")
                    kb.v(f)

                def f(g, h=h):
                    kb.wait(g, "vp")
                    kb.inc(g.dma_start(
                        out=cc_ho_i.ap()[h // 2, (h % 2) * 64:(h % 2) * 64 + 64, :],
                        in_=hbuf), "gz")
                kb.g(f)

            def f(g):
                kb.wait(g, "gz")
                kb.inc(g.collective_compute(
                    "AllGather", AluOpType.bypass, replica_groups=RG,
                    ins=[cc_ho_i.ap().opt()], outs=[cc_ho_o.ap().opt()]), "cc", 1)
                kb.wait(g, "cc")
                for rk in range(2):
                    for tile in range(4):
                        c = rk * 4 + tile
                        kb.inc(g.dma_start(out=big_t[:, c * TSEQ:(c + 1) * TSEQ],
                                           in_=cc_ho_o.ap()[rk, tile]), "gl")
            kb.g(f)

        nag2 = kb.n["gl"]
        for c in range(NCD):
            def f(v, c=c):
                if c == 0:
                    kb.wait(v, "gl", nag2)
                hoall = big_t[:, c * TSEQ:(c + 1) * TSEQ]
                v.tensor_scalar(out=xn_ch[c], in0=hoall[:, 0:TL],
                                scalar1=C["m0col"][:, 0:1], scalar2=None, op0=MUL)
                kb.inc(v.scalar_tensor_tensor(
                    out=xn_ch[c], in0=hoall[:, TL:TSEQ], scalar=C["m1col"][:, 0:1],
                    in1=xn_ch[c], op0=MUL, op1=ADD), "vp")
            kb.v(f)

        with contextlib.ExitStack() as oes:
            wo_sb = oes.enter_context(nc.sbuf_tensor("wo_sb", [128, NCD * DM], BF16)).ap()

            def f(g):
                join_all(g)
                kb.inc(g.dma_start(out=_ft(wo_sb, NCD, DM), in_=_rt(ins["wo"])), "gl")
            kb.g(f)
            nwo = kb.n["gl"]
            for dmc in range(NCD):
                for sl in range(4):
                    def f(t, dmc=dmc, sl=sl):
                        kb.wait(t, "gl", nwo)
                        if not (dmc == 0 and sl == 0):
                            kb.wait(t, "vp")
                        for c in range(NCD):
                            kb.inc(t.matmul(
                                PS[0], wo_sb[:, c * DM + dmc * 128:c * DM + dmc * 128 + 128],
                                xn_t[:, c * TL + sl * 512:c * TL + (sl + 1) * 512],
                                start=(c == 0), stop=(c == NCD - 1)), "tp")
                    kb.t(f)

                    def f(v, dmc=dmc, sl=sl):
                        kb.wait(v, "tp")
                        xsl = x_ch[dmc][:, sl * 512:(sl + 1) * 512]
                        kb.inc(v.scalar_tensor_tensor(
                            out=xsl, in0=PS[0], scalar=C["bo"][:, dmc:dmc + 1],
                            in1=xsl, op0=ADD, op1=ADD), "vp")
                    kb.v(f)
        dump_fm("x2", x_t)

    def ffn_stage(tag):
        barrier5()
        col_stats(x_ch)
        rep_col_grouped(env["statM"], repM)
        for c in range(NCD):
            def f(v, c=c):
                if c == 0:
                    kb.wait(v, "vp")
                kb.inc(v.tensor_tensor(out=xn_ch[c], in0=x_ch[c], in1=repM, op=SUB), "vp")
            kb.v(f)

        with contextlib.ExitStack() as fes:
            fsb = lambda n, s, dt=BF16: fes.enter_context(nc.sbuf_tensor(tag + "_" + n, s, dt)).ap()
            cmax = sqbuf
            qf32 = fsb("qf32", [128, 512], F32)
            tgT = fsb("tgT", [128, NCD * 512])
            tuT = fsb("tuT", [128, NCD * 512])
            tdT = fsb("tdT", [128, DM])
            gatebuf = fsb("gatebuf", [128, 512], F32)
            a2buf = fsb("a2buf", [128, 512], F32)
            a2ld = fsb("a2ld", [128, 512], F32)
            xq2b = fsb("xq2b", [128, 512])
            cm2 = tbuf
            s2rep = fsb("s2rep", [128, TL])
            brep = fsb("brep", [128, TL])

            def f(v):
                kb.wait(v, "vp")
                v.tensor_tensor(out=cmax, in0=xn_ch[0], in1=xn_ch[1], op=MAXOP)
                v.tensor_tensor(out=tbuf, in0=xn_ch[0], in1=xn_ch[1], op=MINOP)
                for c in range(2, NCD):
                    v.tensor_tensor(out=cmax, in0=cmax, in1=xn_ch[c], op=MAXOP)
                    v.tensor_tensor(out=tbuf, in0=tbuf, in1=xn_ch[c], op=MINOP)
                v.tensor_scalar(out=tbuf, in0=tbuf, scalar1=-1.0, scalar2=None, op0=MUL)
                v.tensor_tensor(out=cmax, in0=cmax, in1=tbuf, op=MAXOP)
            kb.v(f)
            for tt in range(16):
                def f(t, tt=tt):
                    kb.wait(t, "vp")
                    kb.inc(t.transpose(PS[3].bitcast(BF16)[:, 0:128],
                                       cmax[:, tt * 128:(tt + 1) * 128], identb), "tp")
                kb.t(f)

                def f(v, tt=tt):
                    kb.wait(v, "tp")
                    kb.inc(v.reduce_max(out=statX[:, tt:tt + 1],
                                        in_=PS[3].bitcast(BF16)[:, 0:128], axis=AXF), "vp")
                kb.v(f)
            def f(s):
                kb.wait(s, "vp")
                s.activation(out=statB, in_=statX, func=AF.Ln)
                kb.inc(s.activation(out=statB, in_=statB, func=AF.Exp, scale=-1.0), "sp")
            kb.s(f)

            def f(v):
                kb.wait(v, "sp")
                kb.inc(v.tensor_scalar(out=statB, in0=statB, scalar1=127.0,
                                       scalar2=None, op0=MUL), "vp")
            kb.v(f)

            def f(s):
                kb.wait(s, "vp")
                s.activation(out=statX, in_=statB, func=AF.Ln)
                kb.inc(s.activation(out=statX, in_=statX, func=AF.Exp, scale=-1.0), "sp")
            kb.s(f)

            def f(v):
                kb.wait(v, "sp")
                kb.inc(v.tensor_tensor(out=statX, in0=statX, in1=env["statR"],
                                       op=MUL), "vp")
            kb.v(f)
            rep_col_grouped(statB, repR, wait_sp=False)
            rep_col_grouped(statX, s2rep, wait_sp=False)

            for c in range(NCD):
                def f(v, c=c):
                    if c == 0:
                        kb.wait(v, "vp")
                    for sl in range(4):
                        ssl = slice(sl * 512, (sl + 1) * 512)
                        v.tensor_tensor(out=qf32, in0=xn_ch[c][:, ssl],
                                        in1=repR[:, ssl], op=MUL)
                        v.tensor_scalar(out=qf32, in0=qf32, scalar1=MAGIC, scalar2=MAGIC,
                                        op0=ADD, op1=SUB)
                        v.tensor_scalar(out=qf32, in0=qf32, scalar1=-128.0, scalar2=127.0,
                                        op0=MAXOP, op1=MINOP)
                        v.tensor_tensor(out=xn_ch[c][:, ssl], in0=qf32,
                                        in1=s2rep[:, ssl], op=MUL)
                kb.v(f)

            def f(v):
                kb.wait(v, "vp")
                v.memset(cm2, 0.0)
                kb.inc(v.memset(sqbuf, 0.0), "vp")
            kb.v(f)
            for ic in range(NCF):
                if ic % 4 == 0:
                    icg = ic // 4

                    def f(sync, icg=icg):
                        kb.wait(sync, "tp")
                        kb.inc(sync.dma_start(
                            out=_ft(tgT, NCD, 512),
                            in_=_rt(scr["tern_g"].ap())[:, :,
                                                        icg * 512:(icg + 1) * 512]), "dw")
                        kb.inc(sync.dma_start(
                            out=_ft(tuT, NCD, 512),
                            in_=_rt(scr["tern_u"].ap())[:, :,
                                                        icg * 512:(icg + 1) * 512]), "dw")
                    kb.y(f)
                ici = ic % 4
                for sl in range(4):
                    def f(t, ici=ici, sl=sl):
                        kb.wait(t, "dw")
                        kb.wait(t, "sp")
                        kb.wait(t, "vp")
                        for c in range(NCD):
                            kb.inc(t.matmul(
                                PS[0], tgT[:, c * 512 + ici * 128:c * 512 + ici * 128 + 128],
                                xn_ch[c][:, sl * 512:(sl + 1) * 512],
                                start=(c == 0), stop=(c == NCD - 1)), "tp")
                        for c in range(NCD):
                            kb.inc(t.matmul(
                                PS[1], tuT[:, c * 512 + ici * 128:c * 512 + ici * 128 + 128],
                                xn_ch[c][:, sl * 512:(sl + 1) * 512],
                                start=(c == 0), stop=(c == NCD - 1)), "tp")
                    kb.t(f)

                    def f(s, sl=sl):
                        kb.wait(s, "tp")
                        kb.inc(s.activation(out=gatebuf, in_=PS[0], func=AF.Silu,
                                            scale=wsc[:, 3:4]), "sp")
                    kb.s(f)

                    def f(v, sl=sl):
                        kb.wait(v, "sp")
                        kb.wait(v, "gz")
                        v.tensor_tensor(out=a2buf, in0=PS[1], in1=gatebuf, op=MUL)
                        v.tensor_tensor(out=cm2[:, sl * 512:(sl + 1) * 512],
                                        in0=cm2[:, sl * 512:(sl + 1) * 512],
                                        in1=a2buf, op=MAXOP)
                        kb.inc(v.tensor_tensor(out=sqbuf[:, sl * 512:(sl + 1) * 512],
                                               in0=sqbuf[:, sl * 512:(sl + 1) * 512],
                                               in1=a2buf, op=MINOP), "vp")
                    kb.v(f)

                    def f(g, ic=ic, sl=sl):
                        kb.wait(g, "vp")
                        kb.inc(g.dma_start(
                            out=scr["act2_s"].ap()[ic][:, sl * 512:(sl + 1) * 512],
                            in_=a2buf), "gz")
                    kb.g(f)

            def f(v):
                v.tensor_scalar(out=sqbuf, in0=sqbuf, scalar1=-1.0, scalar2=None, op0=MUL)
                kb.inc(v.tensor_tensor(out=cm2, in0=cm2, in1=sqbuf, op=MAXOP), "vp")
            kb.v(f)
            for tt in range(16):
                def f(t, tt=tt):
                    if tt == 0:
                        kb.wait(t, "vp")
                    else:
                        kb.wait(t, "vp")
                    kb.inc(t.transpose(PS[3].bitcast(BF16)[:, 0:128],
                                       cm2[:, tt * 128:(tt + 1) * 128], identb), "tp")
                kb.t(f)

                def f(v, tt=tt):
                    kb.wait(v, "tp")
                    kb.inc(v.reduce_max(out=statX[:, tt:tt + 1],
                                        in_=PS[3].bitcast(BF16)[:, 0:128], axis=AXF), "vp")
                kb.v(f)

            def f(s):
                kb.wait(s, "vp")
                s.activation(out=statB, in_=statX, func=AF.Ln)
                kb.inc(s.activation(out=statB, in_=statB, func=AF.Exp, scale=-1.0), "sp")
            kb.s(f)

            def f(v):
                kb.wait(v, "sp")
                v.tensor_scalar(out=statB, in0=statB, scalar1=127.0, scalar2=None, op0=MUL)
                v.tensor_scalar(out=statA, in0=statX, scalar1=1.0 / 127.0, scalar2=None, op0=MUL)
                v.tensor_scalar(out=statA, in0=statA, scalar1=wsc[:, 4:5], scalar2=None, op0=MUL)
                kb.inc(v.tensor_scalar(out=statA, in0=statA, scalar1=wsc[:, 5:6],
                                       scalar2=None, op0=MUL), "vp")
            kb.v(f)
            rep_col_grouped(statB, s2rep, wait_sp=False)
            rep_col_grouped(statA, brep, wait_sp=False)

            for tb in range(4):
                for ic in range(NCF):
                    def f(sync, tb=tb, ic=ic):
                        kb.wait(sync, "vp")
                        kb.wait(sync, "tp")
                        kb.wait(sync, "gz")
                        kb.inc(sync.dma_start(
                            out=a2ld,
                            in_=scr["act2_s"].ap()[ic][:, tb * 512:(tb + 1) * 512]), "dw")
                        kb.inc(sync.dma_start(
                            out=tdT, in_=scr["tern_d"].ap()[ic * 128:(ic + 1) * 128, :]), "dw")
                    kb.y(f)

                    def f(v, tb=tb, ic=ic):
                        kb.wait(v, "dw")
                        kb.wait(v, "tp")
                        v.tensor_tensor(out=qf32[:, 0:512],
                                        in0=a2ld,
                                        in1=s2rep[:, tb * 512:(tb + 1) * 512], op=MUL)
                        v.tensor_scalar(out=qf32[:, 0:512], in0=qf32[:, 0:512],
                                        scalar1=MAGIC, scalar2=MAGIC, op0=ADD, op1=SUB)
                        kb.inc(v.tensor_scalar(out=xq2b, in0=qf32[:, 0:512],
                                               scalar1=-128.0, scalar2=127.0,
                                               op0=MAXOP, op1=MINOP), "vp")
                    kb.v(f)

                    def f(t, tb=tb, ic=ic):
                        kb.wait(t, "vp")
                        for dmc in range(NCD):
                            kb.inc(t.matmul(
                                PS[dmc], tdT[:, dmc * 128:(dmc + 1) * 128],
                                xq2b, start=(ic == 0), stop=(ic == NCF - 1)), "tp")
                    kb.t(f)
                for dmc in range(NCD):
                    def f(v, tb=tb, dmc=dmc):
                        kb.wait(v, "tp")
                        v.tensor_tensor(out=qf32[:, 0:512], in0=PS[dmc],
                                        in1=brep[:, tb * 512:(tb + 1) * 512], op=MUL)
                        xsl = x_ch[dmc][:, tb * 512:(tb + 1) * 512]
                        kb.inc(v.tensor_tensor(out=xsl, in0=xsl, in1=qf32[:, 0:512],
                                               op=ADD), "vp")
                    kb.v(f)
        dump_fm("x_" + tag, x_t)

    def mamba_stage():
        barrier5()
        ln_full(xn_ch, x_ch)
        dump_fm("xn3", xn_t)

        with contextlib.ExitStack() as mes:
            msb = lambda n, s, dt=BF16: mes.enter_context(nc.sbuf_tensor(n, s, dt)).ap()
            halo_m = msb("halo_m", [128, 24], F32)
            halo_bf = msb("halo_bf", [128, 24])
            carryT = msb("carryT", [128, 256], F32)
            stAB = msb("stAB", [128, 2 * 280], F32)
            dtfm = msb("dtfm", [64, TL])
            Bsb = msb("Bsb", [16, TL])
            Csb = msb("Csb", [16, TL])
            dtwT = msb("dtwT", [64, NCI * 128])
            xpT = msb("xpT", [128, NCI * 96])
            giT = msb("giT", [128, NCD * 256])
            otdT = msb("otdT", [128, DM])
            szbuf = msb("szbuf", [128, 512])
            o1buf = msb("o1buf", [128, 512])
            dab = msb("dab", [128, 512])
            dbu = msb("dbu", [128, 512])
            hb = msb("hb", [128, 512])
            gtmp = dbu
            brepb = msb("brepb", [128, 512])
            crepb = msb("crepb", [128, 512])
            ybuf8 = msb("ybuf8", [128, 8 * 512])
            ybufs = ([ybuf8[:, i * 512:(i + 1) * 512] for i in range(8)]
                     + [sqbuf[:, i * 512:(i + 1) * 512] for i in range(4)]
                     + [tbuf[:, i * 512:(i + 1) * 512] for i in range(4)])
            deltab = xn_t.bitcast(mybir.dt.float16)[:, 0:NCI * 512]
            ubuf = xn_t[:, NCI * 512:2 * NCI * 512]

            def f(v):
                kb.wait(v, "vp")
                for c in range(NCD):
                    v.tensor_copy(out=halo_m[:, c * 3:(c + 1) * 3],
                                  in_=xn_ch[c][:, TL - 3:TL])
            kb.v(f)

            def xh_base(dic):
                return dic * 2051

            def dir_pass(d):
                sfx = str(d)
                def f(sync):
                    kb.wait(sync, "tp")
                    kb.inc(sync.dma_start(out=dtwT, in_=scr["dtw_s" + sfx].ap()), "dw")
                    for dic in range(NCI):
                        rsl = slice(dic * 128, (dic + 1) * 128)
                        kb.inc(sync.dma_start(out=xpT[:, dic * 96:dic * 96 + 64],
                                              in_=scr["xpd_s" + sfx].ap()[rsl, :]), "dw")
                        kb.inc(sync.dma_start(out=xpT[:, dic * 96 + 64:dic * 96 + 80],
                                              in_=scr["xpb_s" + sfx].ap()[rsl, :]), "dw")
                        kb.inc(sync.dma_start(out=xpT[:, dic * 96 + 80:dic * 96 + 96],
                                              in_=scr["xpc_s" + sfx].ap()[rsl, :]), "dw")
                kb.y(f)

                for ocg in range(16):
                    def f(sync, ocg=ocg):
                        kb.wait(sync, "tp")
                        kb.inc(sync.dma_start(
                            out=_ft(giT, NCD, 256),
                            in_=_rt(scr["inw_s" + sfx].ap())[:, :,
                                                             ocg * 256:(ocg + 1) * 256]), "dw")
                    kb.y(f)
                    for oci in range(2):
                        oc = ocg * 2 + oci
                        for sl in range(4):
                            def f(t, oci=oci, sl=sl, oc=oc):
                                kb.wait(t, "dw")
                                kb.wait(t, "vp")
                                kb.wait(t, "sp")
                                for c in range(NCD):
                                    kb.inc(t.matmul(
                                        PS[sl % 2],
                                        giT[:, c * 256 + oci * 128:c * 256 + oci * 128 + 128],
                                        xn_ch[c][:, sl * 512:(sl + 1) * 512],
                                        start=(c == 0), stop=(c == NCD - 1)), "tp")
                            kb.t(f)
                            if oc < NCI:
                                def f(v, oc=oc, sl=sl):
                                    kb.wait(v, "tp")
                                    kb.inc(v.tensor_copy(
                                        out=big_t[:, xh_base(oc) + 3 + sl * 512:
                                                  xh_base(oc) + 3 + (sl + 1) * 512],
                                        in_=PS[sl % 2]), "vp")
                                kb.v(f)
                            else:
                                def f(s, oc=oc, sl=sl):
                                    kb.wait(s, "tp")
                                    kb.wait(s, "gz")
                                    kb.inc(s.activation(
                                        out=szbuf, in_=PS[sl % 2], func=AF.Silu), "sp")
                                kb.s(f)

                                def f(g, oc=oc, sl=sl):
                                    kb.wait(g, "sp")
                                    kb.inc(g.dma_start(
                                        out=scr["sz_s" + sfx].ap()[oc - NCI][:, sl * 512:(sl + 1) * 512],
                                        in_=szbuf), "gz")
                                kb.g(f)
                        if oc < NCI:
                            if d == 1:
                                def f(v, oc=oc):
                                    kb.wait(v, "vp")
                                    kb.inc(v.memset(
                                        big_t[:, xh_base(oc):xh_base(oc) + 3], 0.0), "vp")
                                kb.v(f)
                            else:
                                def f(t, oc=oc):
                                    kb.wait(t, "vp")
                                    for c in range(NCD):
                                        kb.inc(t.matmul(
                                            PS[2][:, 0:3],
                                            giT[:, c * 256 + (oc % 2) * 128:c * 256 + (oc % 2) * 128 + 128],
                                            halo_bf[:, c * 3:(c + 1) * 3],
                                            start=(c == 0), stop=(c == NCD - 1)), "tp")
                                kb.t(f)

                                def f(v, oc=oc):
                                    kb.wait(v, "tp")
                                    kb.inc(v.tensor_copy(
                                        out=big_t[:, xh_base(oc):xh_base(oc) + 3],
                                        in_=PS[2][:, 0:3]), "vp")
                                kb.v(f)
                        else:
                            pass

                for dic in range(NCI):
                    def f(v, dic=dic):
                        kb.wait(v, "gz")
                        kb.wait(v, "vp")
                        kb.wait(v, "sp")
                        b = xh_base(dic)
                        cw = MP["convw" + sfx]
                        v.tensor_scalar(out=sqbuf, in0=big_t[:, b:b + TL],
                                        scalar1=cw[:, dic * 4:dic * 4 + 1], scalar2=None, op0=MUL)
                        for k in (1, 2, 3):
                            v.tensor_scalar(out=tbuf, in0=big_t[:, b + k:b + k + TL],
                                            scalar1=cw[:, dic * 4 + k:dic * 4 + k + 1],
                                            scalar2=None, op0=MUL)
                            v.tensor_tensor(out=sqbuf, in0=sqbuf, in1=tbuf, op=ADD)
                    kb.v(f)

                    def f(s, dic=dic):
                        kb.wait(s, "vp")
                        b = xh_base(dic)
                        kb.inc(s.activation(out=big_t[:, b + 3:b + 3 + TL], in_=sqbuf,
                                            func=AF.Silu,
                                            bias=MP["convb" + sfx][:, dic:dic + 1]), "sp")
                    kb.s(f)

                for sl in range(4):
                    def f(t, sl=sl):
                        kb.wait(t, "sp")
                        kb.wait(t, "vp")
                        for dic in range(NCI):
                            b = xh_base(dic)
                            rhs = big_t[:, b + 3 + sl * 512:b + 3 + (sl + 1) * 512]
                            kb.inc(t.matmul(PS[3][0:64, :], xpT[:, dic * 96:dic * 96 + 64],
                                            rhs, start=(dic == 0), stop=(dic == NCI - 1)), "tp")
                            kb.inc(t.matmul(PS[2][0:16, :], xpT[:, dic * 96 + 64:dic * 96 + 80],
                                            rhs, start=(dic == 0), stop=(dic == NCI - 1)), "tp")
                            kb.inc(t.matmul(PS[6][0:16, :], xpT[:, dic * 96 + 80:dic * 96 + 96],
                                            rhs, start=(dic == 0), stop=(dic == NCI - 1)), "tp")
                    kb.t(f)

                    def f(v, sl=sl):
                        kb.wait(v, "tp")
                        v.tensor_copy(out=dtfm[:, sl * 512:(sl + 1) * 512], in_=PS[3][0:64, :])
                        v.tensor_copy(out=Bsb[:, sl * 512:(sl + 1) * 512], in_=PS[2][0:16, :])
                        kb.inc(v.tensor_copy(out=Csb[:, sl * 512:(sl + 1) * 512],
                                             in_=PS[6][0:16, :]), "vp")
                    kb.v(f)

                if d == 1:
                    def f(g):
                        kb.wait(g, "vp")
                        for c in range(NCD):
                            kb.inc(g.dma_start(out=scr["xn3_s"].ap()[c], in_=xn_ch[c]), "gz")
                    kb.g(f)

                    def f(v):
                        kb.wait(v, "gz")
                        kb.inc(v.memset(carryT, 0.0), "vp")
                    kb.v(f)

                for tch in range(4):
                    t0 = tch * 512
                    for dic in range(NCI):
                        def f(t, dic=dic, t0=t0):
                            kb.wait(t, "vp")
                            kb.wait(t, "sp")
                            kb.inc(t.matmul(PS[3], dtwT[:, dic * 128:(dic + 1) * 128],
                                            dtfm[:, t0:t0 + 512], start=True, stop=True), "tp")
                        kb.t(f)

                        def f(s, dic=dic):
                            kb.wait(s, "tp")
                            kb.inc(s.activation(out=PS[4], in_=PS[3], func=AF.Exp,
                                                bias=MP["dtb" + sfx][:, dic:dic + 1]), "sp")
                        kb.s(f)

                        def f(v, dic=dic):
                            kb.wait(v, "sp")
                            kb.inc(v.tensor_scalar(out=PS[4], in0=PS[4], scalar1=1.0,
                                                   scalar2=None, op0=ADD), "vp")
                        kb.v(f)

                        def f(s, dic=dic):
                            kb.wait(s, "vp")
                            kb.inc(s.activation(out=deltab[:, dic * 512:(dic + 1) * 512],
                                                in_=PS[4], func=AF.Ln), "sp")
                        kb.s(f)

                        def f(v, dic=dic, t0=t0):
                            kb.wait(v, "sp")
                            b = xh_base(dic)
                            kb.inc(v.tensor_tensor(
                                out=ubuf[:, dic * 512:(dic + 1) * 512],
                                in0=deltab[:, dic * 512:(dic + 1) * 512],
                                in1=big_t[:, b + 3 + t0:b + 3 + t0 + 512], op=MUL), "vp")
                        kb.v(f)

                    for n in range(NST):
                        def f(t, n=n, t0=t0):
                            kb.wait(t, "vp")
                            kb.inc(t.matmul(PS[2], sel16(n), Bsb[:, t0:t0 + 512],
                                            start=True, stop=True), "tp")
                        kb.t(f)

                        def f(v, n=n):
                            kb.wait(v, "tp")
                            kb.inc(v.tensor_copy(out=brepb, in_=PS[2]), "vp")
                        kb.v(f)

                        def f(t, n=n, t0=t0):
                            kb.wait(t, "vp")
                            kb.inc(t.matmul(PS[2], sel16(n), Csb[:, t0:t0 + 512],
                                            start=True, stop=True), "tp")
                        kb.t(f)

                        def f(v, n=n):
                            kb.wait(v, "tp")
                            kb.inc(v.tensor_copy(out=crepb, in_=PS[2]), "vp")
                        kb.v(f)
                        for dic in range(NCI):
                            def f(s, dic=dic, n=n):
                                kb.wait(s, "vp")
                                kb.inc(s.activation(
                                    out=dab, in_=deltab[:, dic * 512:(dic + 1) * 512],
                                    func=AF.Exp,
                                    scale=A_sb[d][:, dic * NST + n:dic * NST + n + 1]), "sp")
                            kb.s(f)

                            def f(v, dic=dic, n=n):
                                kb.wait(v, "sp")
                                cidx = dic * NST + n
                                v.tensor_tensor(out=dbu, in0=ubuf[:, dic * 512:(dic + 1) * 512],
                                                in1=brepb, op=MUL)
                                v.tensor_tensor_scan(out=hb, data0=dab, data1=dbu,
                                                     initial=carryT[:, cidx:cidx + 1],
                                                     op0=MUL, op1=ADD)
                                v.tensor_copy(out=carryT[:, cidx:cidx + 1], in_=hb[:, 511:512])
                                ysl = ybufs[dic]
                                if n == 0:
                                    kb.inc(v.tensor_tensor(out=ysl, in0=hb, in1=crepb,
                                                           op=MUL), "vp")
                                else:
                                    v.tensor_tensor(out=gtmp, in0=hb, in1=crepb, op=MUL)
                                    kb.inc(v.tensor_tensor(out=ysl, in0=ysl, in1=gtmp,
                                                           op=ADD), "vp")
                            kb.v(f)

                    for dic in range(NCI):
                        def f(sync, dic=dic, tch=tch):
                            kb.wait(sync, "vp")
                            kb.inc(sync.dma_start(
                                out=szbuf[:, 0:512],
                                in_=scr["sz_s" + sfx].ap()[dic][:, tch * 512:(tch + 1) * 512]), "dw")
                        kb.y(f)

                        def f(t, dic=dic, t0=t0):
                            pass
                        def f(v, dic=dic, t0=t0):
                            kb.wait(v, "dw")
                            b = xh_base(dic)
                            ysl = ybufs[dic]
                            v.scalar_tensor_tensor(
                                out=ysl, in0=big_t[:, b + 3 + t0:b + 3 + t0 + 512],
                                scalar=MP["Dp" + sfx][:, dic:dic + 1], in1=ysl,
                                op0=MUL, op1=ADD)
                            kb.inc(v.tensor_tensor(out=ysl, in0=ysl, in1=szbuf[:, 0:512],
                                                   op=MUL), "vp")
                        kb.v(f)

                        def f(sync, dic=dic):
                            kb.wait(sync, "tp")
                            kb.inc(sync.dma_start(
                                out=otdT,
                                in_=scr["outw_s" + sfx].ap()[dic * 128:(dic + 1) * 128, :]), "dw")
                        kb.y(f)

                        def f(t, dic=dic):
                            kb.wait(t, "dw")
                            kb.wait(t, "vp")
                            for dmc in range(NCD):
                                kb.inc(t.matmul(
                                    PS[dmc], otdT[:, dmc * 128:(dmc + 1) * 128],
                                    ybufs[dic],
                                    start=(dic == 0), stop=(dic == NCI - 1)), "tp")
                        kb.t(f)

                    if d == 1:
                        for dmc in range(NCD):
                            def f(v, dmc=dmc):
                                kb.wait(v, "tp")
                                kb.wait(v, "gz")
                                kb.inc(v.tensor_copy(out=o1buf, in_=PS[dmc]), "vp")
                            kb.v(f)

                            def f(g, dmc=dmc, tch=tch):
                                kb.wait(g, "vp")
                                kb.inc(g.dma_start(
                                    out=scr["out1_s"].ap()[dmc][:, tch * 512:(tch + 1) * 512],
                                    in_=o1buf), "gz")
                            kb.g(f)
                    else:
                        rt0 = TL - t0 - 512
                        for dmc in range(NCD):
                            def f(sync, dmc=dmc, rt0=rt0):
                                kb.wait(sync, "vp")
                                kb.inc(sync.dma_start(
                                    out=o1buf,
                                    in_=scr["out1_s"].ap()[dmc][:, rt0:rt0 + 512]), "dw")
                            kb.y(f)

                            def f(v, dmc=dmc, rt0=rt0):
                                kb.wait(v, "tp")
                                kb.wait(v, "dw")
                                v.tensor_tensor(out=o1buf, in0=o1buf,
                                                in1=PS[dmc][:, ::-1], op=ADD)
                                xsl = x_ch[dmc][:, rt0:rt0 + 512]
                                kb.inc(v.tensor_tensor(out=xsl, in0=xsl, in1=o1buf,
                                                       op=ADD), "vp")
                            kb.v(f)

            dir_pass(1)

            def f(g):
                kb.wait(g, "vp")
                kb.wait(g, "tp")
                kb.inc(g.dma_start(out=cc_st_i.ap()[:, 0:256], in_=carryT), "gz")
                kb.inc(g.dma_start(out=cc_st_i.ap()[:, 256:280], in_=halo_m), "gz")
                kb.wait(g, "gz")
                kb.inc(g.collective_compute(
                    "AllGather", AluOpType.bypass, replica_groups=RG,
                    ins=[cc_st_i.ap().opt()], outs=[cc_st_o.ap().opt()]), "cc", 1)
                kb.wait(g, "cc")
                kb.inc(g.dma_start(out=stAB[:, 0:280], in_=cc_st_o.ap()[0]), "gl")
                kb.inc(g.dma_start(out=stAB[:, 280:560], in_=cc_st_o.ap()[1]), "gl")
            kb.g(f)

            def f(v):
                kb.wait(v, "gl")
                v.tensor_scalar(out=stAB[:, 0:280], in0=stAB[:, 0:280],
                                scalar1=C["m1col"][:, 0:1], scalar2=None, op0=MUL)
                v.scalar_tensor_tensor(out=stAB[:, 0:280], in0=stAB[:, 280:560],
                                       scalar=C["m0col"][:, 0:1], in1=stAB[:, 0:280],
                                       op0=MUL, op1=ADD)
                v.tensor_copy(out=carryT, in_=stAB[:, 0:256])
                kb.inc(v.tensor_copy(out=halo_bf, in_=stAB[:, 256:280]), "vp")
            kb.v(f)

            def f(g):
                kb.wait(g, "vp")
                for c in range(NCD):
                    kb.inc(g.dma_start(out=xn_ch[c], in_=scr["xn3_s"].ap()[c]), "gl")
            kb.g(f)
            for c in range(NCD):
                def f(v, c=c):
                    kb.wait(v, "gl")
                    v.tensor_copy(out=sqbuf, in_=xn_ch[c][:, ::-1])
                    kb.inc(v.tensor_copy(out=xn_ch[c], in_=sqbuf), "vp")
                kb.v(f)

            dir_pass(2)

        dump_fm("x4", x_t)

    def finish():
        barrier5()
        with contextlib.ExitStack() as zes:
            xout = zes.enter_context(nc.sbuf_tensor("xout", [128, DM], BF16)).ap()

            def f(g):
                kb.wait(g, "vp")
                kb.wait(g, "tp")
                for c in range(NCD):
                    kb.inc(g.dma_start(out=scr["xfm_s"].ap()[c], in_=x_ch[c]), "gz")
            kb.g(f)
            fmv = scr["xfm_s"].ap().rearrange("c p t -> (c p) t")
            for tt in range(16):
                def f(sync, tt=tt):
                    kb.wait(sync, "gz")
                    kb.wait(sync, "dst")
                    kb.inc(sync.dma_start_transpose(
                        xout, fmv[:, tt * 128:(tt + 1) * 128]), "dw")
                kb.y(f)

                def f(g, tt=tt):
                    kb.wait(g, "dw")
                    kb.inc(g.dma_start(out=out_t[tt * 128:(tt + 1) * 128, :], in_=xout), "dst")
                kb.g(f)

            def f(g):
                kb.wait(g, "dst")
            kb.g(f)

    attn_stage()
    ffn_stage("f1")
    mamba_stage()
    ffn_stage("f2")
    finish()


def kernel(**inputs):
    nc, _ = build_nc()
    in_maps = [_stage_core(inputs, c) for c in range(8)]
    res = run_bass_kernel_spmd(nc, in_maps, core_ids=list(range(8)))
    outs = []
    for c in range(8):
        o = np.asarray(res.results[c]["out"], dtype=np.float32)
        outs.append(o if (c & 1) == 0 else o[::-1])
    return np.stack([np.concatenate([outs[2 * b], outs[2 * b + 1]], 0) for b in range(4)])
